# version 02
# baseline (speedup 1.0000x reference)
import contextlib
import math

import numpy as np
import ml_dtypes

import concourse.bass as bass
import concourse.mybir as mybir
from concourse.bass_utils import run_bass_kernel_spmd

F32 = mybir.dt.float32
BF16 = mybir.dt.bfloat16
I32 = mybir.dt.int32
AF = mybir.ActivationFunctionType
ALU = mybir.AluOpType
AX = mybir.AxisListType

NCORES = 8
SEQS = 2
T = 2048
D = 1024
NT = 8
H = 16
DK = 64
TQ = 512
NQT = T // TQ
CH = 128
FH = 3072
S127 = 1.0 / 127.0
EPS = 1e-5

DEBUG_TAPS = False
DEBUG_PRINT = False



def _rope_perm():
    perm = np.zeros(D, dtype=np.int64)
    for r in range(512):
        h, p = r // 32, r % 32
        perm[r] = h * 64 + 2 * p
        perm[512 + r] = h * 64 + 2 * p + 1
    return perm


def _rope_tables():
    powers = np.arange(0, DK, 2, dtype=np.float32) / DK
    freqs = (1.0 / (10000.0 ** powers)).astype(np.float32)
    t = np.arange(T, dtype=np.float32)
    ang = np.outer(freqs, t).astype(np.float32)
    cosT = np.tile(np.cos(ang).astype(np.float32), (4, 1)).astype(ml_dtypes.bfloat16)
    sinT = np.tile(np.sin(ang).astype(np.float32), (4, 1)).astype(ml_dtypes.bfloat16)
    return cosT, sinT


def _col_tile(v, n):
    return np.ascontiguousarray(np.asarray(v, np.float32).reshape(n, 128).T)


def stage_shared(i):
    perm = _rope_perm()
    cosT, sinT = _rope_tables()
    f32 = lambda a: np.ascontiguousarray(np.asarray(a, dtype=np.float32))
    sel = np.zeros((H, NT * 128), dtype=np.float32)
    for j in range(NT):
        for p in range(128):
            sel[2 * j + (1 if p >= 64 else 0), j * 128 + p] = 1.0
    e4 = np.zeros((128, 128), dtype=np.float32)
    for p in range(128):
        e4[p, 32 * (p // 32)] = 1.0
    d = {
        "wq0T": f32(np.asarray(i["L0_attn_wq"])[perm].T),
        "wk0T": f32(np.asarray(i["L0_attn_wk"])[perm].T),
        "wv0T": f32(np.asarray(i["L0_attn_wv"]).T),
        "wo0T": f32(np.asarray(i["L0_attn_wo"]).T),
        "cn1": f32(i["L0_ffn_cn_w1"]), "cn2": f32(i["L0_ffn_cn_w2"]),
        "cn3": f32(i["L0_ffn_cn_w3"]), "cn4": f32(i["L0_ffn_cn_w4"]),
        "cn1b": _col_tile(i["L0_ffn_cn_b1"], 8),
        "cn2b": _col_tile(i["L0_ffn_cn_b2"], 4),
        "cn3b": _col_tile(i["L0_ffn_cn_b3"], 2),
        "cn4b": f32(i["L0_ffn_cn_b4"]).reshape(1, 1),
        "cn4c": _col_tile(i["L0_ffn_cn_w4"].reshape(-1), 2),
        "gc3c": _col_tile(i["L0_ffn_gc_w3"].reshape(-1), 2),
        "gc1": f32(i["L0_ffn_gc_w1"]), "gc2": f32(i["L0_ffn_gc_w2"]),
        "gc3": f32(i["L0_ffn_gc_w3"]),
        "gc1b": _col_tile(i["L0_ffn_gc_b1"], 4),
        "gc2b": _col_tile(i["L0_ffn_gc_b2"], 2),
        "gc3b": f32(i["L0_ffn_gc_b3"]).reshape(1, 1),
        "ar": f32(i["L0_ffn_adaptive_range"]).reshape(1, 1),
        "wq1T": f32(np.asarray(i["L1_attn_Wq"]).T),
        "wk1T": f32(np.asarray(i["L1_attn_Wk"]).T),
        "wv1T": f32(np.asarray(i["L1_attn_Wv"]).T),
        "wo1T": f32(np.asarray(i["L1_attn_Wo"]).T),
        "bq1": _col_tile(i["L1_attn_bq"], 8),
        "bk1": _col_tile(i["L1_attn_bk"], 8),
        "bv1r": f32(i["L1_attn_bv"]).reshape(1, D),
        "bo1": _col_tile(i["L1_attn_bo"], 8),
        "gam": f32(i["L1_attn_gamma"]).reshape(H, 1),
        "wg1T": np.ascontiguousarray(np.asarray(i["L1_ffn_w_gate"], np.int32).T),
        "wu1T": np.ascontiguousarray(np.asarray(i["L1_ffn_w_up"], np.int32).T),
        "wd1T": np.ascontiguousarray(np.asarray(i["L1_ffn_w_down"], np.int32).T),
        "cosT": cosT, "sinT": sinT,
        "tri": np.triu(np.ones((CH, CH), np.float32)).astype(ml_dtypes.bfloat16),
        "sel": sel.astype(ml_dtypes.bfloat16),
        "e4sel": e4.astype(ml_dtypes.bfloat16),
    }
    return d



class Em:
    def __init__(self, nc, sems):
        self.nc = nc
        self.sems = sems
        self.cur = None
        self.ct = {}
        self.lastwait = {}

    def start(self, engine):
        self.cur = engine
        self.ct = {k: 0 for k in self.sems}
        self.lastwait = {}

    def op(self, eng, fn, waits=(), inc=None):
        if inc is not None and inc[0] in ("ld", "wl", "st"):
            waits = list(waits) + [(inc[0], self.ct[inc[0]])]
        if eng == self.cur:
            e = getattr(self.nc, eng)
            for st, v in waits:
                if v > 0 and self.lastwait.get(st, -1) < v:
                    e.wait_ge(self.sems[st], v)
                    self.lastwait[st] = v
            ins = fn(e)
            if inc is not None and ins is not None:
                ins.then_inc(self.sems[inc[0]], inc[1])
        if inc is not None:
            self.ct[inc[0]] += inc[1]

    def waitall(self, eng, sems, extra=()):
        vals = [(st, self.ct[st]) for st in sems] + list(extra)
        if eng == self.cur:
            e = getattr(self.nc, eng)
            for st, v in vals:
                if v > 0 and self.lastwait.get(st, -1) < v:
                    e.wait_ge(self.sems[st], v)
                    self.lastwait[st] = v

    def vflush(self):
        self.op("vector", lambda e: e.engine_nop(), inc=("vm", 1))
        self.waitall("vector", ("vm",))

    def barrier(self):
        for eng in ("tensor", "scalar", "vector", "gpsimd", "sync"):
            self.waitall(eng, ("tm", "sm", "vm", "ld"))


class Guard:
    def __init__(self):
        self.w = {}

    def use(self, sem, ct):
        self.w[sem] = max(self.w.get(sem, 0), ct)

    def waits(self):
        return list(self.w.items())

    def reset(self):
        self.w = {}



def build_nc():
    nc = bass.Bass("TRN2", target_bir_lowering=False, detect_race_conditions=False,
                   debug=DEBUG_PRINT)
    dram = {}

    def din(name, shape, dt=F32):
        dram[name] = nc.dram_tensor(name, shape, dt, kind="ExternalInput").ap()

    din("xin", [SEQS, D, T])
    for n in ("wq0T", "wk0T", "wv0T", "wo0T", "wq1T", "wk1T", "wv1T", "wo1T"):
        din(n, [D, D])
    din("cn1", [D, D]); din("cn2", [D, 512]); din("cn3", [512, 256]); din("cn4", [256, 1])
    din("cn1b", [128, 8]); din("cn2b", [128, 4]); din("cn3b", [128, 2]); din("cn4b", [1, 1])
    din("gc1", [D, 512]); din("gc2", [512, 256]); din("gc3", [256, 1])
    din("cn4c", [128, 2]); din("gc3c", [128, 2])
    din("gc1b", [128, 4]); din("gc2b", [128, 2]); din("gc3b", [1, 1]); din("ar", [1, 1])
    din("bq1", [128, 8]); din("bk1", [128, 8]); din("bv1r", [1, D]); din("bo1", [128, 8])
    din("gam", [H, 1])
    din("wg1T", [D, FH], I32); din("wu1T", [D, FH], I32); din("wd1T", [FH, D], I32)
    din("cosT", [128, T], BF16); din("sinT", [128, T], BF16)
    din("tri", [CH, CH], BF16); din("sel", [H, NT * 128], BF16)
    din("e4sel", [128, 128], BF16)

    out = nc.dram_tensor("out", [SEQS, D, T], F32, kind="ExternalOutput").ap()

    taps = {}
    if DEBUG_TAPS:
        taps["tap_rows"] = nc.dram_tensor("tap_rows", [1, 4 * T], F32, kind="ExternalOutput").ap()
        taps["tap_K"] = nc.dram_tensor("tap_K", [128, NT * T], F32, kind="ExternalOutput").ap()
        taps["tap_V"] = nc.dram_tensor("tap_V", [128, 16 * 1040], F32, kind="ExternalOutput").ap()
        taps["tap_state"] = nc.dram_tensor("tap_state", [D, T], F32, kind="ExternalOutput").ap()
        taps["tap_xnb"] = nc.dram_tensor("tap_xnb", [128, NT * T], F32, kind="ExternalOutput").ap()
        taps["tap_state2"] = nc.dram_tensor("tap_state2", [D, T], F32, kind="ExternalOutput").ap()
        taps["tap_x1"] = nc.dram_tensor("tap_x1", [D, T], F32, kind="ExternalOutput").ap()
        taps["tap_dbg"] = nc.dram_tensor("tap_dbg", [128, 48], F32, kind="ExternalOutput").ap()

    RH = {}
    for n in ("wq0T", "wk0T", "wv0T", "wo0T", "cn1", "cn2", "wq1T", "wk1T",
              "wv1T", "wo1T", "wg1T", "wu1T", "cn3", "gc1", "gc2", "wd1T"):
        RH[n] = dram[n].rearrange("(kt p) n -> p kt n", p=128)
    RH64 = {"wo0T": dram["wo0T"].rearrange("(h p) n -> p h n", p=64)}
    xre_src = dram["xin"].rearrange("s (j p) t -> s p j t", p=128)

    sb = {}

    def fixed(name, shape, dt):
        sb[name] = nc.alloc_sbuf_tensor(name, shape, dt)

    fixed("stateF", [128, NT * T], F32)
    fixed("triS", [CH, CH], BF16)
    fixed("mrow", [1, T], F32)
    fixed("rrow", [1, T], F32)
    fixed("srow", [128, 1024], F32)
    fixed("onesb", [128, 8], BF16)
    fixed("onesf", [1, 128], F32)
    fixed("ones64", [128, 64], F32)
    fixed("b1S", [128, 8], F32)
    fixed("b2S", [128, 4], F32)
    fixed("b3S", [128, 2], F32)
    fixed("gb1S", [128, 4], F32)
    fixed("gb2S", [128, 2], F32)
    fixed("bq1S", [128, 8], F32)
    fixed("bk1S", [128, 8], F32)
    fixed("bo1S", [128, 8], F32)
    fixed("smalls", [128, 16], F32)
    fixed("gamS", [H, 2], F32)
    fixed("kprS", [128, 64], F32)
    fixed("e4S", [128, 128], BF16)
    fixed("negb", [128, 128], BF16)
    fixed("gmS", [128, 16], F32)
    fixed("gmB", [128, 16], BF16)

    SB_TOP = 229376
    rem = nc.sbuf_bytes_remaining
    arena_base = (SB_TOP - rem + 31) & ~31
    arena_bytes = SB_TOP - arena_base

    def at(name, off, shape, dt):
        nbytes = shape[1] * {F32: 4, BF16: 2, I32: 4}[dt]
        assert off + nbytes <= arena_bytes, (name, off, nbytes, arena_bytes)
        sb[name] = nc.alloc_sbuf_tensor_at(name, shape, dt, offset=arena_base + off)

    at("Kb", 0, [128, NT * T], BF16)
    at("Vb", 32768, [128, 16 * 1040], BF16)
    at("xnq", 66048, [128, NT * TQ], BF16)
    at("Qb", 74240, [128, NT * TQ], BF16)
    at("wbig", 74240, [128, 4096], BF16)
    at("attnO", 82432, [64, H * TQ], BF16)
    at("qraw", 82432, [128, NT * TQ], BF16)
    at("Pb", 98816, [128, 2048], BF16)
    at("oscr", 102912, [64, 512], BF16)
    at("wslA", 103936, [128, 1024], BF16)
    at("wslB", 105984, [128, 1024], BF16)
    at("wo64A", 108032, [64, 2048], BF16)
    at("c8b", 108032, [128, 2048], BF16)
    at("scrC", 112128, [128, 512], F32)
    at("sqb", 114176, [128, 1024], BF16)
    at("cosS", 116224, [128, T], BF16)
    at("sinS", 120320, [128, T], BF16)
    endC = 124416

    at("xre", 0, [128, NT * TQ], F32)
    at("stT", 16384, [128, NT * TQ], F32)
    at("outb", 32768, [128, NT * TQ], BF16)
    at("h1b", 40960, [128, NT * TQ], BF16)
    at("h2b", 49152, [128, 4 * TQ], BF16)
    at("h3b", 53248, [128, 2 * TQ], BF16)
    at("cn1S", 55296, [128, 8 * 1024], BF16)
    at("cn2S", 71680, [128, 8 * 512], BF16)
    at("cn3S", 79872, [128, 4 * 256], BF16)
    at("cn4S", 81920, [128, 2], BF16)
    at("twsc", 81952, [128, 2 * TQ], F32)
    at("gcsS", 86048, [128, 1024], BF16)
    endD = 88096

    at("xnb2", 0, [128, NT * T], BF16)
    at("K2b", 32768, [128, 4 * T], BF16)
    at("V2b", 49152, [128, 16 * TQ], BF16)
    at("Q2b", 65536, [128, 4 * TQ], BF16)
    at("aO1", 69632, [128, NT * TQ], BF16)
    at("P1b", 77824, [128, 2048], BF16)
    at("scr1", 81920, [128, 512], F32)
    at("wsl1A", 83968, [128, 1024], BF16)
    at("wsl1B", 86016, [128, 1024], BF16)
    at("wsl1C", 88064, [128, 2048], BF16)
    at("wsl1D", 92160, [128, 2048], BF16)
    at("sqb1", 96256, [128, 1024], BF16)
    at("decS", 98304, [H, T], BF16)
    at("selS", 102400, [H, NT * 128], BF16)
    at("bvrow", 104448, [1, D], F32)
    at("bvrS", 108544, [128, D], BF16)
    endE = 110592

    NRING = 9
    for r in range(NRING):
        at(f"rg{r}", 32768 + r * 6144, [128, 1024], BF16)
        at(f"ru{r}", 32768 + r * 6144 + 2048, [128, 1024], BF16)
        at(f"rd{r}", 32768 + r * 6144 + 4096, [128, 1024], BF16)
    at("hblk", 32768 + NRING * 6144, [128, 6 * TQ], BF16)
    at("fscr", 32768 + NRING * 6144 + 6144, [128, TQ], BF16)
    endF = 32768 + NRING * 6144 + 6144 + 1024

    for e in (endC, endD, endE, endF):
        assert e <= arena_bytes, (e, arena_bytes)

    PS = [nc.alloc_psum_tensor(f"PS{i}", [128, 1024], F32) for i in range(4)]

    es = contextlib.ExitStack()
    sems = {n: es.enter_context(nc.semaphore(n))
            for n in ("ld", "wl", "tm", "sm", "vm", "st", "tp", "gp")}
    E = Em(nc, sems)
    ctx = dict(dram=dram, RH=RH, RH64=RH64, sb=sb, PS=PS, out=out, taps=taps,
               xre_src=xre_src)

    block = es.enter_context(nc.Block())

    def runner(name):
        def body(_):
            E.start(name)
            program(E, ctx)
        return body

    block.tensor(runner("tensor"))
    block.scalar(runner("scalar"))
    block.vector(runner("vector"))
    block.gpsimd(runner("gpsimd"))
    block.sync(runner("sync"))
    es.close()
    return nc



def program(E, c):
    dram, RH, RH64, sb, PS = c["dram"], c["RH"], c["RH64"], c["sb"], c["PS"]
    out, taps, xre_src = c["out"], c["taps"], c["xre_src"]

    stF = sb["stateF"]; Kb = sb["Kb"]; Vb = sb["Vb"]; wbig = sb["wbig"]
    xnq = sb["xnq"]; Qb = sb["Qb"]; attnO = sb["attnO"]; qraw = sb["qraw"]
    Pb = sb["Pb"]; oscr = sb["oscr"]; scrC = sb["scrC"]; sqb = sb["sqb"]
    wslA = sb["wslA"]; wslB = sb["wslB"]; wo64A = sb["wo64A"]
    mrow = sb["mrow"]; rrow = sb["rrow"]; srow = sb["srow"]
    cosS = sb["cosS"]; sinS = sb["sinS"]; triS = sb["triS"]
    onesb = sb["onesb"]; onesf = sb["onesf"]; ones64 = sb["ones64"]
    smalls = sb["smalls"]; gamS = sb["gamS"]
    kprS = sb["kprS"]; e4S = sb["e4S"]; negb = sb["negb"]
    gmS = sb["gmS"]; gmB = sb["gmB"]; c8b = sb["c8b"]
    selS = sb["selS"]
    decS = sb["decS"]; bvrS = sb["bvrS"]; bvrow = sb["bvrow"]
    xre = sb["xre"]; stT = sb["stT"]; outb = sb["outb"]
    h1b = sb["h1b"]; h2b = sb["h2b"]; h3b = sb["h3b"]
    cn1S = sb["cn1S"]; cn2S = sb["cn2S"]; cn3S = sb["cn3S"]; cn4S = sb["cn4S"]
    twsc = sb["twsc"]; gcsS = sb["gcsS"]
    xnb2 = sb["xnb2"]; K2b = sb["K2b"]; V2b = sb["V2b"]; Q2b = sb["Q2b"]
    aO1 = sb["aO1"]; P1b = sb["P1b"]; scr1 = sb["scr1"]; sqb1 = sb["sqb1"]
    wsl1 = [sb["wsl1A"], sb["wsl1B"], sb["wsl1C"], sb["wsl1D"]]
    ringG = [sb[f"rg{r}"] for r in range(9)]
    ringU = [sb[f"ru{r}"] for r in range(9)]
    ringD = [sb[f"rd{r}"] for r in range(9)]
    hblk = sb["hblk"]; fscr = sb["fscr"]

    psG = [Guard() for _ in range(4)]
    bufG = {}

    def G(name):
        if name not in bufG:
            bufG[name] = Guard()
        return bufG[name]

    def dma(dst, src):
        E.op("sync", lambda e, d=dst, s=src: e.dma_start(out=d, in_=s), inc=("ld", 16))

    dma(triS[:], dram["tri"])
    dma(sb["b1S"][:], dram["cn1b"]); dma(sb["b2S"][:], dram["cn2b"])
    dma(sb["b3S"][:], dram["cn3b"])
    dma(sb["gb1S"][:], dram["gc1b"]); dma(sb["gb2S"][:], dram["gc2b"])
    dma(sb["bq1S"][:], dram["bq1"]); dma(sb["bk1S"][:], dram["bk1"])
    dma(sb["bo1S"][:], dram["bo1"])
    dma(e4S[:], dram["e4sel"])
    dma(smalls[0:1, 0:1], dram["cn4b"]); dma(smalls[0:1, 1:2], dram["gc3b"])
    dma(smalls[0:1, 2:3], dram["ar"]); dma(gamS[:, 0:1], dram["gam"])
    ld_setup = E.ct["ld"]

    E.op("vector", lambda e: e.memset(onesb[:], 1.0), inc=("vm", 1))
    E.op("vector", lambda e: e.memset(onesf[:], 1.0), inc=("vm", 1))
    E.op("vector", lambda e: e.memset(ones64[:], 1.0), inc=("vm", 1))
    E.op("vector", lambda e: e.memset(smalls[0:1, 14:15], EPS), inc=("vm", 1))
    E.op("vector", lambda e: e.memset(negb[:], -1.0), inc=("vm", 1))
    vm_ones = E.ct["vm"]

    E.op("scalar", lambda e: e.activation(gamS[:, 1:2], gamS[:, 0:1], AF.Ln),
         waits=[("ld", ld_setup)], inc=("sm", 1))
    sm_lng = E.ct["sm"]

    def wload(buf, rhname, col0, ncols, kt=8, waits=()):
        src = RH[rhname][:, 0:kt, col0:col0 + ncols]
        dst = buf[:, 0:kt * ncols].rearrange("p (k n) -> p k n", n=ncols)
        E.op("gpsimd", lambda e, d=dst, s=src: e.dma_start(out=d, in_=s),
             waits=waits, inc=("wl", 16))
        return E.ct["wl"]

    for s in range(SEQS):
        g = G("cosS")
        E.op("sync", lambda e: e.dma_start(out=cosS[:], in_=dram["cosT"]),
             waits=g.waits(), inc=("ld", 16))
        g.reset()
        g = G("sinS")
        E.op("sync", lambda e: e.dma_start(out=sinS[:], in_=dram["sinT"]),
             waits=g.waits(), inc=("ld", 16))
        g.reset()
        for j in range(NT):
            g = G(f"stF{j}")
            E.op("sync", lambda e, j=j: e.dma_start(
                out=stF[:, j * T:(j + 1) * T],
                in_=dram["xin"][s, j * 128:(j + 1) * 128, :]),
                waits=g.waits(), inc=("ld", 16))
            g.reset()
        ld_x = E.ct["ld"]

        def ln_stats(buf, gname):
            for ch in range(4):
                for j in range(NT):
                    gsq = G(gname)
                    E.op("scalar", lambda e, j=j, ch=ch: e.activation(
                        buf[:, 0:512], stF[:, j * T + ch * 512:j * T + (ch + 1) * 512],
                        AF.Identity), waits=[("ld", ld_x)] + gsq.waits(), inc=("sm", 1))
                    E.op("scalar", lambda e, j=j, ch=ch: e.activation(
                        buf[:, 512:1024], stF[:, j * T + ch * 512:j * T + (ch + 1) * 512],
                        AF.Square), inc=("sm", 1))
                    sm_cast = E.ct["sm"]
                    E.op("tensor", lambda e, j=j: e.matmul(
                        PS[3][0:1, 0:512], onesb[:, 0:1], buf[:, 0:512],
                        start=(j == 0), stop=(j == NT - 1)),
                        waits=[("sm", sm_cast), ("vm", vm_ones)] +
                        (psG[3].waits() if j == 0 else []), inc=("tm", 1))
                    E.op("tensor", lambda e, j=j: e.matmul(
                        PS[3][0:1, 512:1024], onesb[:, 0:1], buf[:, 512:1024],
                        start=(j == 0), stop=(j == NT - 1)), inc=("tm", 1))
                    gsq.use("tm", E.ct["tm"])
                psG[3].reset()
                tm_st = E.ct["tm"]
                E.op("vector", lambda e, ch=ch: e.tensor_scalar_mul(
                    mrow[0:1, ch * 512:(ch + 1) * 512], PS[3][0:1, 0:512], 1.0 / D),
                    waits=[("tm", tm_st)], inc=("vm", 1))
                E.op("vector", lambda e, ch=ch: e.tensor_scalar_mul(
                    srow[0:1, 0:512], PS[3][0:1, 512:1024], 1.0 / D), inc=("vm", 1))
                E.op("vector", lambda e, ch=ch: e.tensor_mul(
                    srow[0:1, 512:1024], mrow[0:1, ch * 512:(ch + 1) * 512],
                    mrow[0:1, ch * 512:(ch + 1) * 512]), inc=("vm", 1))
                E.op("vector", lambda e: e.tensor_sub(
                    srow[0:1, 0:512], srow[0:1, 0:512], srow[0:1, 512:1024]),
                    inc=("vm", 1))
                vm_var = E.ct["vm"]
                psG[3].use("vm", vm_var)
                E.op("scalar", lambda e: e.activation(
                    srow[0:1, 0:512], srow[0:1, 0:512], AF.Sqrt,
                    bias=smalls[0:1, 14:15]),
                    waits=[("vm", vm_var), ("vm", vm_ones)], inc=("sm", 1))
                sm_sqrt = E.ct["sm"]
                E.op("vector", lambda e, ch=ch: e.reciprocal(
                    rrow[0:1, ch * 512:(ch + 1) * 512], srow[0:1, 0:512]),
                    waits=[("sm", sm_sqrt)])
                E.op("vector", lambda e: e.engine_nop(), inc=("vm", 1))
                E.waitall("vector", ("vm",))

        ln_stats(sqb, "sqb")
        vm_rows = E.ct["vm"]
        E.barrier()

        if DEBUG_TAPS and s == 0:
            E.op("sync", lambda e: e.dma_start(out=taps["tap_rows"][0:1, 0:T], in_=mrow[:]),
                 waits=[("vm", vm_rows)], inc=("st", 16))
            E.op("sync", lambda e: e.dma_start(out=taps["tap_rows"][0:1, T:2 * T], in_=rrow[:]),
                 inc=("st", 16))

        def reps(cols0, extra_waits=()):
            E.op("tensor", lambda e: e.matmul(
                PS[3][:, 0:512], onesf[:], mrow[0:1, cols0:cols0 + 512],
                start=True, stop=True),
                waits=[("vm", vm_rows)] + psG[3].waits() + list(extra_waits),
                inc=("tm", 1))
            E.op("tensor", lambda e: e.matmul(
                PS[3][:, 512:1024], onesf[:], rrow[0:1, cols0:cols0 + 512],
                start=True, stop=True), inc=("tm", 1))
            psG[3].reset()
            return E.ct["tm"]

        def derive_xn(dstbuf, dst_off, cols0, tm_rep, gname):
            for j in range(NT):
                gx = G(gname)
                E.op("vector", lambda e, j=j, cols0=cols0: e.tensor_sub(
                    scrC[:, 0:512], stF[:, j * T + cols0:j * T + cols0 + 512],
                    PS[3][:, 0:512]),
                    waits=[("tm", tm_rep)] + G("scrC").waits() + (gx.waits() if j == 0 else []),
                    inc=("vm", 1))
                E.op("vector", lambda e, j=j: e.tensor_tensor(
                    dstbuf[:, dst_off + j * 512:dst_off + (j + 1) * 512],
                    scrC[:, 0:512], PS[3][:, 512:1024], ALU.mult), inc=("vm", 1))
            G(gname).reset()
            psG[3].use("vm", E.ct["vm"])
            return E.ct["vm"]

        E.op("vector", lambda e: e.memset(gmS[:, 0:8], 0.0), inc=("vm", 1))
        for tt in range(16):
            E.op("vector", lambda e, tt=tt: e.memset(
                Vb[:, tt * 1040:(tt + 1) * 1040].rearrange(
                    "p (h c) -> p h c", c=65)[:, :, 64:65], 1.0), inc=("vm", 1))
        for tc in range(4):
            tm_rep = reps(tc * 512)
            vm_xnq = derive_xn(xnq, 0, tc * 512, tm_rep, "xnq")
            for j in range(NT):
                E.op("vector", lambda e, j=j: e.reduce_sum(
                    smalls[:, 8:9], xnq[:, j * 512:(j + 1) * 512], axis=AX.X))
                E.op("vector", lambda e: e.engine_nop(), inc=("vm", 1))
                E.waitall("vector", ("vm",))
                E.op("vector", lambda e, j=j: e.tensor_add(
                    gmS[:, j:j + 1], gmS[:, j:j + 1], smalls[:, 8:9]), inc=("vm", 1))
            for jo in range(NT):
                wb = wslA if jo % 2 == 0 else wslB
                gw = G("wslA" if jo % 2 == 0 else "wslB")
                wl_k = wload(wb, "wk0T", jo * 128, 128, waits=gw.waits())
                pq = PS[0] if jo % 2 == 0 else PS[1]
                pg = psG[0] if jo % 2 == 0 else psG[1]
                for kt in range(8):
                    E.op("tensor", lambda e, kt=kt, wb=wb, pq=pq: e.matmul(
                        pq[:, 0:512], wb[:, kt * 128:(kt + 1) * 128],
                        xnq[:, kt * 512:(kt + 1) * 512],
                        start=(kt == 0), stop=(kt == 7)),
                        waits=[("wl", wl_k), ("vm", vm_xnq)] +
                        (pg.waits() if kt == 0 else []),
                        inc=(("tm", 1) if kt == 7 else None))
                gw.use("tm", E.ct["tm"])
                pg.reset()
                tm_k = E.ct["tm"]
                gq = G("attnO")
                E.op("scalar", lambda e, jo=jo, pq=pq: e.activation(
                    qraw[:, jo * 512:(jo + 1) * 512], pq[:, 0:512], AF.Identity),
                    waits=[("tm", tm_k)] + (gq.waits() if jo == 0 else []),
                    inc=("sm", 1))
                pg.use("sm", E.ct["sm"])
            G("attnO").reset()
            sm_kraw = E.ct["sm"]

            for je in range(4):
                jo2 = je + 4
                gs = G("sqb")
                E.op("vector", lambda e, je=je, tc=tc: e.tensor_tensor(
                    sqb[:, 0:512], qraw[:, je * 512:(je + 1) * 512],
                    cosS[:, tc * 512:(tc + 1) * 512], ALU.mult),
                    waits=[("sm", sm_kraw)] + gs.waits() + G("Kb").waits(),
                    inc=("vm", 1))
                E.op("vector", lambda e, jo2=jo2, tc=tc: e.tensor_tensor(
                    sqb[:, 512:1024], qraw[:, jo2 * 512:(jo2 + 1) * 512],
                    sinS[:, tc * 512:(tc + 1) * 512], ALU.mult), inc=("vm", 1))
                E.op("vector", lambda e, je=je, tc=tc: e.tensor_sub(
                    Kb[:, je * T + tc * 512:je * T + (tc + 1) * 512],
                    sqb[:, 0:512], sqb[:, 512:1024]), inc=("vm", 1))
                E.op("vector", lambda e, je=je, tc=tc: e.tensor_tensor(
                    sqb[:, 0:512], qraw[:, je * 512:(je + 1) * 512],
                    sinS[:, tc * 512:(tc + 1) * 512], ALU.mult), inc=("vm", 1))
                E.op("vector", lambda e, jo2=jo2, tc=tc: e.tensor_tensor(
                    sqb[:, 512:1024], qraw[:, jo2 * 512:(jo2 + 1) * 512],
                    cosS[:, tc * 512:(tc + 1) * 512], ALU.mult), inc=("vm", 1))
                E.op("vector", lambda e, jo2=jo2, tc=tc: e.tensor_add(
                    Kb[:, jo2 * T + tc * 512:jo2 * T + (tc + 1) * 512],
                    sqb[:, 0:512], sqb[:, 512:1024]), inc=("vm", 1))
                gs.use("vm", E.ct["vm"])
            G("attnO").use("vm", E.ct["vm"])

            for half in range(2):
                gw = G("wbig")
                wl_v = wload(wbig, "wv0T", half * 512, 512, waits=gw.waits())
                for ts_ in range(4):
                    tsub = tc * 4 + ts_
                    pq = PS[0] if ts_ % 2 == 0 else PS[1]
                    pg = psG[0] if ts_ % 2 == 0 else psG[1]
                    for kt in range(8):
                        E.op("tensor", lambda e, kt=kt, ts_=ts_, pq=pq: e.matmul(
                            pq[:, 0:512],
                            xnq[:, kt * 512 + ts_ * 128:kt * 512 + (ts_ + 1) * 128],
                            wbig[:, kt * 512:(kt + 1) * 512],
                            start=(kt == 0), stop=(kt == 7)),
                            waits=[("wl", wl_v), ("vm", vm_xnq)] +
                            (pg.waits() if kt == 0 else []),
                            inc=(("tm", 1) if kt == 7 else None))
                    pg.reset()
                    tm_v = E.ct["tm"]
                    dst = Vb[:, tsub * 1040:(tsub + 1) * 1040].rearrange(
                        "p (h c) -> p h c", c=65)[:, half * 8:(half + 1) * 8, 0:64]
                    E.op("scalar", lambda e, pq=pq, dst=dst: e.activation(
                        dst, pq[:, 0:512].rearrange("p (h c) -> p h c", c=64),
                        AF.Identity), waits=[("tm", tm_v)], inc=("sm", 1))
                    pg.use("sm", E.ct["sm"])
                gw.use("tm", E.ct["tm"])
        vm_endB = E.ct["vm"]

        for jq in range(4):
            for tc4 in range(4):
                for w2, jt in enumerate((jq, jq + 4)):
                    gs = G("sqb")
                    E.op("scalar", lambda e, jt=jt, tc4=tc4: e.activation(
                        sqb[:, 0:512], Kb[:, jt * T + tc4 * 512:jt * T + (tc4 + 1) * 512],
                        AF.Square), waits=[("vm", vm_endB), ("ld", ld_setup)] + gs.waits(),
                        inc=("sm", 1))
                    sm_sq = E.ct["sm"]
                    E.op("tensor", lambda e, w2=w2: e.matmul(
                        PS[3][:, 0:512], e4S[:], sqb[:, 0:512],
                        start=(w2 == 0), stop=(w2 == 1)),
                        waits=[("sm", sm_sq)] + (psG[3].waits() if w2 == 0 else []),
                        inc=("tm", 1))
                    gs.use("tm", E.ct["tm"])
                psG[3].reset()
                tm_kn = E.ct["tm"]
                for c4 in range(4):
                    E.op("vector", lambda e, jq=jq, tc4=tc4, c4=c4: e.reduce_max(
                        kprS[:, jq * 16 + tc4 * 4 + c4:jq * 16 + tc4 * 4 + c4 + 1],
                        PS[3][:, c4 * 128:(c4 + 1) * 128], axis=AX.X),
                        waits=[("tm", tm_kn)])
                E.op("vector", lambda e: e.engine_nop(), inc=("vm", 1))
                E.waitall("vector", ("vm",))
                psG[3].use("vm", E.ct["vm"])
            for sh in (1, 2, 4, 8):
                E.vflush()
                E.op("vector", lambda e, jq=jq, sh=sh: e.tensor_max(
                    kprS[:, jq * 16 + sh:jq * 16 + 16],
                    kprS[:, jq * 16 + sh:jq * 16 + 16],
                    kprS[:, jq * 16:jq * 16 + 16 - sh]), inc=("vm", 1))
        vm_kpr0 = E.ct["vm"]
        E.op("scalar", lambda e: e.activation(kprS[:], kprS[:], AF.Sqrt),
             waits=[("vm", vm_kpr0)], inc=("sm", 1))
        sm_kpr = E.ct["sm"]

        E.op("vector", lambda e: e.tensor_scalar_mul(gmS[:, 0:8], gmS[:, 0:8], 1.0 / T),
             waits=[("vm", vm_endB)], inc=("vm", 1))
        E.vflush()
        E.op("vector", lambda e: e.tensor_copy(gmB[:, 0:8], gmS[:, 0:8]), inc=("vm", 1))
        E.vflush()
        vm_gm = E.ct["vm"]
        for dj in range(4):
            gw = G("gcs")
            wl_g = wload(gcsS, "gc1", dj * 128, 128, waits=gw.waits())
            for kt in range(8):
                E.op("tensor", lambda e, kt=kt: e.matmul(
                    PS[3][:, 0:1], gcsS[:, kt * 128:(kt + 1) * 128], gmB[:, kt:kt + 1],
                    start=(kt == 0), stop=(kt == 7)),
                    waits=[("wl", wl_g), ("vm", vm_gm)] + (psG[3].waits() if kt == 0 else []),
                    inc=(("tm", 1) if kt == 7 else None))
            gw.use("tm", E.ct["tm"])
            psG[3].reset()
            E.op("scalar", lambda e, dj=dj: e.activation(
                gmB[:, 8 + dj:9 + dj], PS[3][:, 0:1], AF.Gelu,
                bias=sb["gb1S"][:, dj:dj + 1]), waits=[("tm", E.ct["tm"])], inc=("sm", 1))
            psG[3].use("sm", E.ct["sm"])
        sm_g1 = E.ct["sm"]
        for dj in range(2):
            gw = G("gcs")
            wl_g = wload(gcsS, "gc2", dj * 128, 128, kt=4, waits=gw.waits())
            for kt in range(4):
                E.op("tensor", lambda e, kt=kt: e.matmul(
                    PS[3][:, 0:1], gcsS[:, kt * 128:(kt + 1) * 128],
                    gmB[:, 8 + kt:9 + kt], start=(kt == 0), stop=(kt == 3)),
                    waits=[("wl", wl_g), ("sm", sm_g1)] + (psG[3].waits() if kt == 0 else []),
                    inc=(("tm", 1) if kt == 3 else None))
            gw.use("tm", E.ct["tm"])
            psG[3].reset()
            E.op("scalar", lambda e, dj=dj: e.activation(
                gmB[:, 12 + dj:13 + dj], PS[3][:, 0:1], AF.Gelu,
                bias=sb["gb2S"][:, dj:dj + 1]), waits=[("tm", E.ct["tm"])], inc=("sm", 1))
            psG[3].use("sm", E.ct["sm"])
        sm_g2 = E.ct["sm"]
        gw = G("gcs")
        E.op("gpsimd", lambda e: e.dma_start(out=gcsS[:, 0:2], in_=dram["gc3c"]),
             waits=gw.waits(), inc=("wl", 16))
        wl_g = E.ct["wl"]
        for kt in range(2):
            E.op("tensor", lambda e, kt=kt: e.matmul(
                PS[3][0:1, 0:1], gcsS[:, kt:kt + 1], gmB[:, 12 + kt:13 + kt],
                start=(kt == 0), stop=(kt == 1)),
                waits=[("wl", wl_g), ("sm", sm_g2)] + (psG[3].waits() if kt == 0 else []),
                inc=(("tm", 1) if kt == 1 else None))
        gw.use("tm", E.ct["tm"])
        psG[3].reset()
        E.op("scalar", lambda e: e.activation(
            smalls[0:1, 3:4], PS[3][0:1, 0:1], AF.Sigmoid, bias=smalls[0:1, 1:2]),
            waits=[("tm", E.ct["tm"]), ("ld", ld_setup)], inc=("sm", 1))
        psG[3].use("sm", E.ct["sm"])
        sm_gf = E.ct["sm"]
        E.op("vector", lambda e: e.tensor_mul(
            smalls[0:1, 6:7], smalls[0:1, 3:4], smalls[0:1, 2:3]),
            waits=[("sm", sm_gf)], inc=("vm", 1))
        E.op("vector", lambda e: e.tensor_scalar(
            smalls[0:1, 7:8], smalls[0:1, 2:3], -1.0, 0.1, ALU.mult, ALU.add),
            inc=("vm", 1))
        E.vflush()
        E.op("vector", lambda e: e.tensor_scalar(
            smalls[0:1, 4:5], smalls[0:1, 6:7], 1.4, smalls[0:1, 7:8],
            ALU.mult, ALU.add), inc=("vm", 1))
        E.op("vector", lambda e: e.tensor_scalar_mul(
            smalls[0:1, 5:6], smalls[0:1, 2:3], 0.6), inc=("vm", 1))
        E.vflush()
        vm_ab = E.ct["vm"]
        if DEBUG_TAPS and s == 0:
            E.op("sync", lambda e: e.dma_start(
                out=taps["tap_dbg"][:, 0:16], in_=gmS[:]),
                waits=[("vm", vm_ab), ("sm", E.ct["sm"])], inc=("st", 16))
            E.op("gpsimd", lambda e: e.dma_start(
                out=taps["tap_dbg"][:, 16:32], in_=gmB[:]), inc=("tp", 16))

        E.barrier()

        if DEBUG_TAPS and s == 0:
            E.op("gpsimd", lambda e: e.dma_start(out=taps["tap_K"], in_=Kb[:]),
                 waits=[("vm", E.ct["vm"]), ("sm", E.ct["sm"])], inc=("tp", 16))
            E.op("gpsimd", lambda e: e.dma_start(out=taps["tap_V"], in_=Vb[:]),
                 inc=("tp", 16))

        for qi in range(NQT):
            tm_rep = reps(qi * 512)
            vm_xq = derive_xn(xnq, 0, qi * 512, tm_rep, "xnq")
            for je in range(4):
                for sl, jj in enumerate((je, je + 4)):
                    wb = wslA if sl == 0 else wslB
                    gw = G("wslA" if sl == 0 else "wslB")
                    wl_q = wload(wb, "wq0T", jj * 128, 128, waits=gw.waits())
                    for kt in range(8):
                        E.op("tensor", lambda e, kt=kt, wb=wb: e.matmul(
                            PS[3][:, 0:512], wb[:, kt * 128:(kt + 1) * 128],
                            xnq[:, kt * 512:(kt + 1) * 512],
                            start=(kt == 0), stop=(kt == 7)),
                            waits=[("wl", wl_q), ("vm", vm_xq)] +
                            (psG[3].waits() if kt == 0 else []),
                            inc=(("tm", 1) if kt == 7 else None))
                    gw.use("tm", E.ct["tm"])
                    psG[3].reset()
                    tm_q = E.ct["tm"]
                    gq = G("attnO")
                    E.op("scalar", lambda e, sl=sl: e.activation(
                        qraw[:, sl * 512:(sl + 1) * 512], PS[3][:, 0:512], AF.Identity),
                        waits=[("tm", tm_q)] + gq.waits(), inc=("sm", 1))
                    psG[3].use("sm", E.ct["sm"])
                sm_qraw = E.ct["sm"]
                if DEBUG_PRINT and s == 0 and qi == 0 and je == 0:
                    E.op("vector", lambda e: e.print("xnq", xnq[0:1, 0:4]),
                         waits=[("sm", sm_qraw)])
                    E.op("vector", lambda e: e.print("qraw0", qraw[0:1, 0:4]))
                    E.op("vector", lambda e: e.print("qraw1", qraw[0:1, 512:516]))
                    E.op("vector", lambda e: e.print("ps3a", PS[3][0:1, 0:4]))
                    E.op("vector", lambda e: e.print("ps3b", PS[3][0:1, 512:516]))
                    E.op("vector", lambda e: e.print("qrawP", qraw[0:4, 0:1],
                                                     qraw[0:4, 513:514]))
                    E.op("vector", lambda e: e.print("xnqT", xnq[0:1, 512:514],
                                                     xnq[0:1, 1024:1026],
                                                     xnq[0:1, 3584:3586]))
                gs = G("sqb")
                gQ = G("Qb")
                E.op("vector", lambda e, qi=qi: e.tensor_tensor(
                    sqb[:, 0:512], qraw[:, 0:512], cosS[:, qi * 512:(qi + 1) * 512],
                    ALU.mult), waits=[("sm", sm_qraw)] + gs.waits() + gQ.waits(),
                    inc=("vm", 1))
                E.op("vector", lambda e, qi=qi: e.tensor_tensor(
                    sqb[:, 512:1024], qraw[:, 512:1024],
                    sinS[:, qi * 512:(qi + 1) * 512], ALU.mult), inc=("vm", 1))
                E.op("vector", lambda e, je=je: e.tensor_sub(
                    Qb[:, je * 512:(je + 1) * 512], sqb[:, 0:512], sqb[:, 512:1024]),
                    inc=("vm", 1))
                E.op("vector", lambda e, qi=qi: e.tensor_tensor(
                    sqb[:, 0:512], qraw[:, 0:512], sinS[:, qi * 512:(qi + 1) * 512],
                    ALU.mult), inc=("vm", 1))
                E.op("vector", lambda e, qi=qi: e.tensor_tensor(
                    sqb[:, 512:1024], qraw[:, 512:1024],
                    cosS[:, qi * 512:(qi + 1) * 512], ALU.mult), inc=("vm", 1))
                E.op("vector", lambda e, je=je: e.tensor_add(
                    Qb[:, (je + 4) * 512:(je + 5) * 512], sqb[:, 0:512],
                    sqb[:, 512:1024]), inc=("vm", 1))
                gs.use("vm", E.ct["vm"])
                G("attnO").use("vm", E.ct["vm"])
            vm_Q = E.ct["vm"]
            G("Qb").reset()

            for jq in range(4):
                for w2, jt in enumerate((jq, jq + 4)):
                    gs = G("sqb")
                    E.op("scalar", lambda e, jt=jt: e.activation(
                        sqb[:, 0:512], Qb[:, jt * 512:(jt + 1) * 512], AF.Square),
                        waits=[("vm", vm_Q)] + gs.waits(), inc=("sm", 1))
                    sm_s = E.ct["sm"]
                    E.op("tensor", lambda e, w2=w2: e.matmul(
                        PS[3][:, 0:512], e4S[:], sqb[:, 0:512],
                        start=(w2 == 0), stop=(w2 == 1)),
                        waits=[("sm", sm_s)] + (psG[3].waits() if w2 == 0 else []),
                        inc=("tm", 1))
                    gs.use("tm", E.ct["tm"])
                psG[3].reset()
                tm_qn = E.ct["tm"]
                E.op("scalar", lambda e: e.activation(
                    scrC[:, 0:512], PS[3][:, 0:512], AF.Sqrt),
                    waits=[("tm", tm_qn)] + G("scrC").waits(), inc=("sm", 1))
                sm_qn = E.ct["sm"]
                psG[3].use("sm", sm_qn)
                gc8 = G("wo64c8")
                E.op("vector", lambda e, jq=jq, qi=qi: e.tensor_scalar_mul(
                    c8b[:, jq * 512:(jq + 1) * 512], scrC[:, 0:512],
                    kprS[:, jq * 16 + 4 * qi + 3:jq * 16 + 4 * qi + 4]),
                    waits=[("sm", sm_qn), ("sm", sm_kpr)] + gc8.waits(), inc=("vm", 1))
                G("scrC").use("vm", E.ct["vm"])
            vm_c8 = E.ct["vm"]
            G("wo64c8").reset()
            if DEBUG_PRINT == 2 and s == 0 and qi == 0:
                E.op("vector", lambda e: e.print("km2", km2S[0:1, 0:8]),
                     waits=[("sm", sm_cb)])
                E.op("vector", lambda e: e.print("cb", cbS[0:1, 0:4]))
                E.op("vector", lambda e: e.print("Q0", sb["Qb"][0:1, 0:6]))
                E.op("vector", lambda e: e.print("K0", sb["Kb"][0:1, 0:6]))

            for hp in range(8):
                jq = hp // 2
                je, jo2 = jq, jq + 4
                ra = 32 * ((2 * hp) % 4)
                rb = 32 * ((2 * hp + 1) % 4)
                nch_q = 4 * (qi + 1)

                def emit_st(cc):
                    par = cc % 2
                    pq = PS[par]
                    pg = psG[par]
                    diag = cc >= 4 * qi
                    o = (cc - 4 * qi) * 128 if diag else 0
                    wts = pg.waits() + [("vm", vm_Q), ("vm", vm_c8)]
                    for rr, half_t in ((ra, 0), (rb, 1)):
                        co = half_t * 512
                        E.op("tensor", lambda e, rr=rr, cc=cc, o=o, pq=pq, co=co: e.matmul(
                            pq[:, co + o:co + 512],
                            Kb[rr:rr + 32, je * T + cc * 128:je * T + (cc + 1) * 128],
                            Qb[rr:rr + 32, je * 512 + o:(je + 1) * 512],
                            start=True, stop=False, tile_position=(rr, 0)),
                            waits=wts)
                        E.op("tensor", lambda e, rr=rr, cc=cc, o=o, pq=pq, co=co: e.matmul(
                            pq[:, co + o:co + 512],
                            Kb[rr:rr + 32, jo2 * T + cc * 128:jo2 * T + (cc + 1) * 128],
                            Qb[rr:rr + 32, jo2 * 512 + o:(jo2 + 1) * 512],
                            start=False, stop=False, tile_position=(rr, 0)))
                        E.op("tensor", lambda e, rr=rr, o=o, pq=pq, co=co: e.matmul(
                            pq[:, co + o:co + 512],
                            negb[rr:rr + 1, 0:128],
                            c8b[rr:rr + 1, jq * 512 + o:(jq + 1) * 512],
                            start=False, stop=True, tile_position=(rr, 0)),
                            inc=("tm", 1))
                    pg.reset()
                    return E.ct["tm"], o, diag, par

                def emit_expv(cc, tm_s, o, diag, par, last):
                    pq = PS[par]
                    pg = psG[par]
                    gP = G(f"P{par}")
                    if o == 0:
                        E.op("scalar", lambda e, par=par, pq=pq: e.activation(
                            Pb[:, par * 1024:(par + 1) * 1024], pq[:, 0:1024], AF.Exp,
                            scale=0.125),
                            waits=[("tm", tm_s)] + gP.waits(), inc=("sm", 1))
                    else:
                        E.op("scalar", lambda e, par=par, pq=pq, o=o: e.activation(
                            Pb[:, par * 1024 + o:par * 1024 + 512], pq[:, o:512],
                            AF.Exp, scale=0.125),
                            waits=[("tm", tm_s)] + gP.waits(), inc=("sm", 1))
                        E.op("scalar", lambda e, par=par, pq=pq, o=o: e.activation(
                            Pb[:, par * 1024 + 512 + o:par * 1024 + 1024],
                            pq[:, 512 + o:1024], AF.Exp, scale=0.125), inc=("sm", 1))
                    sm_p = E.ct["sm"]
                    pg.use("sm", sm_p)
                    vmw = []
                    if diag:
                        E.op("vector", lambda e, par=par, o=o: e.tensor_tensor(
                            Pb[:, par * 1024 + o:par * 1024 + o + 128],
                            Pb[:, par * 1024 + o:par * 1024 + o + 128], triS[:],
                            ALU.mult), waits=[("sm", sm_p)], inc=("vm", 1))
                        E.op("vector", lambda e, par=par, o=o: e.tensor_tensor(
                            Pb[:, par * 1024 + 512 + o:par * 1024 + 512 + o + 128],
                            Pb[:, par * 1024 + 512 + o:par * 1024 + 512 + o + 128],
                            triS[:], ALU.mult), inc=("vm", 1))
                        vmw = [("vm", E.ct["vm"])]
                    pvw = [("sm", sm_p)] + vmw
                    E.op("tensor", lambda e, cc=cc, o=o, par=par, last=last: e.matmul(
                        PS[2][0:65, o:512],
                        Vb[:, cc * 1040 + (2 * hp) * 65:cc * 1040 + (2 * hp) * 65 + 65],
                        Pb[:, par * 1024 + o:par * 1024 + 512],
                        start=(cc == 0), stop=last),
                        waits=pvw + (psG[2].waits() if cc == 0 else []), inc=None)
                    E.op("tensor", lambda e, cc=cc, o=o, par=par, last=last: e.matmul(
                        PS[2][0:65, 512 + o:1024],
                        Vb[:, cc * 1040 + (2 * hp + 1) * 65:cc * 1040 + (2 * hp + 1) * 65 + 65],
                        Pb[:, par * 1024 + 512 + o:par * 1024 + 1024],
                        start=(cc == 0), stop=last), inc=("tm", 1))
                    gP.use("tm", E.ct["tm"])
                    if cc == 0:
                        psG[2].reset()

                prev = None
                for cc in range(nch_q):
                    cur = (cc,) + emit_st(cc)
                    if prev is not None:
                        emit_expv(prev[0], prev[1], prev[2], prev[3], prev[4],
                                  prev[0] == nch_q - 1)
                    prev = cur
                emit_expv(prev[0], prev[1], prev[2], prev[3], prev[4], True)
                tm_pv = E.ct["tm"]
                if DEBUG_PRINT == 2 and s == 0 and qi == 0 and hp == 0:
                    E.op("vector", lambda e: e.print("P", Pb[0:1, 0:6]),
                         waits=[("tm", tm_pv)])
                    E.op("vector", lambda e: e.print("l", PS[2][64:65, 0:6]))
                    E.op("vector", lambda e: e.print("O", PS[2][0:1, 0:6]))

                for hh in range(2):
                    h = 2 * hp + hh
                    co = hh * 512
                    E.op("vector", lambda e, co=co: e.reciprocal(
                        srow[64:65, 0:512], PS[2][64:65, co:co + 512]),
                        waits=[("tm", tm_pv)] + G("srow64").waits())
                    E.op("vector", lambda e: e.memset(smalls[0:1, 12:13], 0.0),
                         inc=("vm", 1))
                    vm_rc = E.ct["vm"]
                    E.op("tensor", lambda e: e.matmul(
                        PS[3][0:64, 0:512], ones64[64:65, 0:64], srow[64:65, 0:512],
                        start=True, stop=True, tile_position=(64, 0)),
                        waits=[("vm", vm_rc)] + psG[3].waits(), inc=("tm", 1))
                    tm_rc = E.ct["tm"]
                    psG[3].reset()
                    G("srow64").use("tm", tm_rc)
                    E.op("scalar", lambda e, co=co: e.activation(
                        oscr[0:64, 0:512], PS[2][0:64, co:co + 512], AF.Identity),
                        waits=[("tm", tm_pv), ("vm", vm_rc)] + G("oscr").waits(),
                        inc=("sm", 1))
                    sm_o = E.ct["sm"]
                    gA = G("attnO")
                    E.op("vector", lambda e, h=h: e.tensor_tensor(
                        attnO[0:64, h * 512:(h + 1) * 512], oscr[0:64, 0:512],
                        PS[3][0:64, 0:512], ALU.mult),
                        waits=[("sm", sm_o), ("tm", tm_rc)], inc=("vm", 1))
                    G("oscr").use("vm", E.ct["vm"])
                    psG[3].use("vm", E.ct["vm"])
                    if hh == 1:
                        psG[2].use("sm", E.ct["sm"])
                        psG[2].use("vm", E.ct["vm"])
            vm_attnO = E.ct["vm"]
            G("wo64c8").use("tm", E.ct["tm"])

            for dj in range(NT):
                wb = wo64A
                gw = G("wo64c8")
                srcw_ = RH64["wo0T"][:, :, dj * 128:(dj + 1) * 128]
                dstw = wb[:, 0:2048].rearrange("p (h n) -> p h n", n=128)
                E.op("gpsimd", lambda e, d=dstw, srcw=srcw_: e.dma_start(out=d, in_=srcw),
                     waits=gw.waits(), inc=("wl", 16))
                wl_o = E.ct["wl"]
                for h in range(H):
                    E.op("tensor", lambda e, h=h, wb=wb: e.matmul(
                        PS[3][:, 0:512], wb[0:64, h * 128:(h + 1) * 128],
                        attnO[0:64, h * 512:(h + 1) * 512],
                        start=(h == 0), stop=(h == H - 1)),
                        waits=[("wl", wl_o), ("vm", vm_attnO)] +
                        (psG[3].waits() if h == 0 else []),
                        inc=(("tm", 1) if h == H - 1 else None))
                gw.use("tm", E.ct["tm"])
                psG[3].reset()
                tm_wo = E.ct["tm"]
                E.op("vector", lambda e, dj=dj, qi=qi: e.tensor_add(
                    stF[:, dj * T + qi * 512:dj * T + (qi + 1) * 512], PS[3][:, 0:512],
                    stF[:, dj * T + qi * 512:dj * T + (qi + 1) * 512]),
                    waits=[("tm", tm_wo)], inc=("vm", 1))
                psG[3].use("vm", E.ct["vm"])
            G("attnO").use("tm", E.ct["tm"])
        G("cosS").use("vm", E.ct["vm"])
        G("sinS").use("vm", E.ct["vm"])
        E.barrier()

        if DEBUG_TAPS and s == 0:
            for j in range(NT):
                E.op("sync", lambda e, j=j: e.dma_start(
                    out=taps["tap_state"][j * 128:(j + 1) * 128, :],
                    in_=stF[:, j * T:(j + 1) * T]),
                    waits=[("vm", E.ct["vm"]), ("tm", E.ct["tm"])], inc=("st", 16))

        E.op("gpsimd", lambda e: e.dma_start(
            out=cn1S[:].rearrange("p (k n) -> p k n", n=1024), in_=RH["cn1"][:, 0:8, :]),
            waits=G("cn1S").waits(), inc=("wl", 16))
        E.op("gpsimd", lambda e: e.dma_start(
            out=cn2S[:].rearrange("p (k n) -> p k n", n=512), in_=RH["cn2"][:, 0:8, :]),
            waits=G("cn2S").waits(), inc=("wl", 16))
        E.op("gpsimd", lambda e: e.dma_start(
            out=cn3S[:].rearrange("p (k n) -> p k n", n=256), in_=RH["cn3"][:, 0:4, :]),
            waits=G("cn3S").waits(), inc=("wl", 16))
        E.op("gpsimd", lambda e: e.dma_start(
            out=cn4S[:], in_=dram["cn4c"]),
            waits=G("cn4S").waits(), inc=("wl", 16))
        wl_cn = E.ct["wl"]

        for tt in range(NQT):
            gx = G("xre")
            E.op("sync", lambda e, tt=tt: e.dma_start(
                out=xre[:].rearrange("p (j t) -> p j t", t=512),
                in_=xre_src[s][:, :, tt * 512:(tt + 1) * 512]),
                waits=gx.waits(), inc=("ld", 16))
            ld_xre = E.ct["ld"]
            tm_rep = reps(tt * 512)
            gst = G("stT")
            gob = G("outb")
            for j in range(NT):
                E.op("vector", lambda e, j=j: e.tensor_sub(
                    scrC[:, 0:512], xre[:, j * 512:(j + 1) * 512], PS[3][:, 0:512]),
                    waits=[("tm", tm_rep), ("ld", ld_xre)] + G("scrC").waits() +
                    (gst.waits() if j == 0 else []), inc=("vm", 1))
                E.op("vector", lambda e, j=j: e.tensor_tensor(
                    stT[:, j * 512:(j + 1) * 512], scrC[:, 0:512], PS[3][:, 512:1024],
                    ALU.mult), inc=("vm", 1))
                E.op("vector", lambda e, j=j: e.tensor_copy(
                    outb[:, j * 512:(j + 1) * 512], stT[:, j * 512:(j + 1) * 512]),
                    waits=(gob.waits() if j == 0 else []), inc=("vm", 1))
            gst.reset(); gob.reset()
            psG[3].use("vm", E.ct["vm"])
            G("xre").use("vm", E.ct["vm"])

            for cyc in range(3):
                vm_in = E.ct["vm"]
                for dj in range(NT):
                    pq = PS[dj % 2]; pg = psG[dj % 2]
                    for kt in range(8):
                        E.op("tensor", lambda e, dj=dj, kt=kt, pq=pq: e.matmul(
                            pq[:, 0:512],
                            cn1S[:, kt * 1024 + dj * 128:kt * 1024 + (dj + 1) * 128],
                            outb[:, kt * 512:(kt + 1) * 512],
                            start=(kt == 0), stop=(kt == 7)),
                            waits=[("wl", wl_cn), ("vm", vm_in)] +
                            (pg.waits() if kt == 0 else []),
                            inc=(("tm", 1) if kt == 7 else None))
                    pg.reset()
                    tm_h = E.ct["tm"]
                    gh = G("h1b")
                    E.op("scalar", lambda e, dj=dj, pq=pq: e.activation(
                        h1b[:, dj * 512:(dj + 1) * 512], pq[:, 0:512], AF.Gelu,
                        bias=sb["b1S"][:, dj:dj + 1]),
                        waits=[("tm", tm_h)] + (gh.waits() if dj == 0 else []),
                        inc=("sm", 1))
                    pg.use("sm", E.ct["sm"])
                G("h1b").reset()
                sm_h1 = E.ct["sm"]
                for dj in range(4):
                    pq = PS[dj % 2]; pg = psG[dj % 2]
                    for kt in range(8):
                        E.op("tensor", lambda e, dj=dj, kt=kt, pq=pq: e.matmul(
                            pq[:, 0:512],
                            cn2S[:, kt * 512 + dj * 128:kt * 512 + (dj + 1) * 128],
                            h1b[:, kt * 512:(kt + 1) * 512],
                            start=(kt == 0), stop=(kt == 7)),
                            waits=[("sm", sm_h1)] + (pg.waits() if kt == 0 else []),
                            inc=(("tm", 1) if kt == 7 else None))
                    pg.reset()
                    tm_h = E.ct["tm"]
                    gh = G("h2b")
                    E.op("scalar", lambda e, dj=dj, pq=pq: e.activation(
                        h2b[:, dj * 512:(dj + 1) * 512], pq[:, 0:512], AF.Gelu,
                        bias=sb["b2S"][:, dj:dj + 1]),
                        waits=[("tm", tm_h)] + (gh.waits() if dj == 0 else []),
                        inc=("sm", 1))
                    pg.use("sm", E.ct["sm"])
                G("h2b").reset()
                sm_h2 = E.ct["sm"]
                G("h1b").use("tm", E.ct["tm"])
                for dj in range(2):
                    pq = PS[dj % 2]; pg = psG[dj % 2]
                    for kt in range(4):
                        E.op("tensor", lambda e, dj=dj, kt=kt, pq=pq: e.matmul(
                            pq[:, 0:512],
                            cn3S[:, kt * 256 + dj * 128:kt * 256 + (dj + 1) * 128],
                            h2b[:, kt * 512:(kt + 1) * 512],
                            start=(kt == 0), stop=(kt == 3)),
                            waits=[("sm", sm_h2)] + (pg.waits() if kt == 0 else []),
                            inc=(("tm", 1) if kt == 3 else None))
                    pg.reset()
                    tm_h = E.ct["tm"]
                    gh = G("h3b")
                    E.op("scalar", lambda e, dj=dj, pq=pq: e.activation(
                        h3b[:, dj * 512:(dj + 1) * 512], pq[:, 0:512], AF.Gelu,
                        bias=sb["b3S"][:, dj:dj + 1]),
                        waits=[("tm", tm_h)] + (gh.waits() if dj == 0 else []),
                        inc=("sm", 1))
                    pg.use("sm", E.ct["sm"])
                G("h3b").reset()
                sm_h3 = E.ct["sm"]
                G("h2b").use("tm", E.ct["tm"])
                for kt in range(2):
                    E.op("tensor", lambda e, kt=kt: e.matmul(
                        PS[3][0:1, 0:512], cn4S[:, kt:kt + 1],
                        h3b[:, kt * 512:(kt + 1) * 512],
                        start=(kt == 0), stop=(kt == 1)),
                        waits=[("sm", sm_h3)] + (psG[3].waits() if kt == 0 else []),
                        inc=(("tm", 1) if kt == 1 else None))
                psG[3].reset()
                tm_s4 = E.ct["tm"]
                G("h3b").use("tm", tm_s4)
                E.op("scalar", lambda e: e.activation(
                    srow[0:1, 0:512], PS[3][0:1, 0:512], AF.Sigmoid,
                    bias=smalls[0:1, 0:1]), waits=[("tm", tm_s4)], inc=("sm", 1))
                sm_tf = E.ct["sm"]
                psG[3].use("sm", sm_tf)
                E.op("vector", lambda e: e.tensor_scalar(
                    srow[0:1, 512:1024], srow[0:1, 0:512], smalls[0:1, 5:6],
                    smalls[0:1, 4:5], ALU.mult, ALU.add),
                    waits=[("sm", sm_tf), ("vm", vm_ab)], inc=("vm", 1))
                vm_cp = E.ct["vm"]
                E.op("tensor", lambda e: e.matmul(
                    PS[3][:, 512:1024], onesf[:], srow[0:1, 512:1024],
                    start=True, stop=True),
                    waits=[("vm", vm_cp)] + psG[3].waits(), inc=("tm", 1))
                tm_cp = E.ct["tm"]
                psG[3].reset()
                for pr in range(4):
                    jA, jB = pr, pr + 4
                    pos = pr < 2
                    E.op("vector", lambda e, jB=jB: e.tensor_tensor(
                        twsc[:, 0:512], PS[3][:, 512:1024],
                        stT[:, jB * 512:(jB + 1) * 512], ALU.mult),
                        waits=[("tm", tm_cp)], inc=("vm", 1))
                    E.op("vector", lambda e, jA=jA: e.tensor_tensor(
                        twsc[:, 512:1024], PS[3][:, 512:1024],
                        stT[:, jA * 512:(jA + 1) * 512], ALU.mult), inc=("vm", 1))
                    if pos:
                        E.op("vector", lambda e, jA=jA: e.tensor_add(
                            stT[:, jA * 512:(jA + 1) * 512],
                            stT[:, jA * 512:(jA + 1) * 512], twsc[:, 0:512]),
                            inc=("vm", 1))
                        E.op("vector", lambda e, jB=jB: e.tensor_sub(
                            stT[:, jB * 512:(jB + 1) * 512],
                            stT[:, jB * 512:(jB + 1) * 512], twsc[:, 512:1024]),
                            inc=("vm", 1))
                    else:
                        E.op("vector", lambda e, jA=jA: e.tensor_sub(
                            stT[:, jA * 512:(jA + 1) * 512],
                            stT[:, jA * 512:(jA + 1) * 512], twsc[:, 0:512]),
                            inc=("vm", 1))
                        E.op("vector", lambda e, jB=jB: e.tensor_add(
                            stT[:, jB * 512:(jB + 1) * 512],
                            stT[:, jB * 512:(jB + 1) * 512], twsc[:, 512:1024]),
                            inc=("vm", 1))
                psG[3].use("vm", E.ct["vm"])
                if cyc < 2:
                    for j in range(NT):
                        E.op("vector", lambda e, j=j: e.tensor_copy(
                            outb[:, j * 512:(j + 1) * 512],
                            stT[:, j * 512:(j + 1) * 512]), inc=("vm", 1))
            for j in range(NT):
                E.op("vector", lambda e, j=j, tt=tt: e.tensor_add(
                    stF[:, j * T + tt * 512:j * T + (tt + 1) * 512],
                    stF[:, j * T + tt * 512:j * T + (tt + 1) * 512],
                    stT[:, j * 512:(j + 1) * 512]), inc=("vm", 1))
            G("stT").use("vm", E.ct["vm"])
            G("outb").use("tm", E.ct["tm"])
        G("cn1S").use("tm", E.ct["tm"])
        G("cn2S").use("tm", E.ct["tm"])
        G("cn3S").use("tm", E.ct["tm"])
        G("cn4S").use("tm", E.ct["tm"])
        E.barrier()
        if DEBUG_TAPS and s == 0:
            for j in range(NT):
                E.op("sync", lambda e, j=j: e.dma_start(
                    out=taps["tap_x1"][j * 128:(j + 1) * 128, :],
                    in_=stF[:, j * T:(j + 1) * T]),
                    waits=[("vm", E.ct["vm"]), ("tm", E.ct["tm"])], inc=("st", 16))
            E.op("sync", lambda e: e.dma_start(
                out=taps["tap_rows"][0:1, 2 * T:2 * T + 16], in_=sb["smalls"][0:1, 0:16]),
                inc=("st", 16))

        ln_stats(sqb1, "sqb1")
        vm_rows = E.ct["vm"]
        for tc in range(4):
            tm_rep = reps(tc * 512)
            for j in range(NT):
                gx = G("xnb2")
                E.op("vector", lambda e, j=j, tc=tc: e.tensor_sub(
                    scrC[:, 0:512], stF[:, j * T + tc * 512:j * T + (tc + 1) * 512],
                    PS[3][:, 0:512]),
                    waits=[("tm", tm_rep)] + G("scrC").waits() +
                    (gx.waits() if j == 0 and tc == 0 else []), inc=("vm", 1))
                E.op("vector", lambda e, j=j, tc=tc: e.tensor_tensor(
                    xnb2[:, j * T + tc * 512:j * T + (tc + 1) * 512], scrC[:, 0:512],
                    PS[3][:, 512:1024], ALU.mult), inc=("vm", 1))
            psG[3].use("vm", E.ct["vm"])
        G("xnb2").reset()
        vm_xnb2 = E.ct["vm"]

        g = G("selS")
        E.op("sync", lambda e: e.dma_start(out=selS[:], in_=dram["sel"]),
             waits=g.waits(), inc=("ld", 16))
        g.reset()
        g = G("bvrow")
        E.op("sync", lambda e: e.dma_start(out=bvrow[:], in_=dram["bv1r"]),
             waits=g.waits(), inc=("ld", 16))
        g.reset()
        ld_tab = E.ct["ld"]
        for hf in range(2):
            E.op("tensor", lambda e, hf=hf: e.matmul(
                PS[3][:, hf * 512:(hf + 1) * 512], onesf[:],
                bvrow[0:1, hf * 512:(hf + 1) * 512], start=True, stop=True),
                waits=[("vm", vm_ones), ("ld", ld_tab)] +
                (psG[3].waits() if hf == 0 else []), inc=("tm", 1))
        psG[3].reset()
        G("bvrow").use("tm", E.ct["tm"])
        E.op("scalar", lambda e: e.activation(bvrS[:], PS[3][:, 0:1024], AF.Identity),
             waits=[("tm", E.ct["tm"])], inc=("sm", 1))
        psG[3].use("sm", E.ct["sm"])
        for tc in range(4):
            gi = G("scr1")
            E.op("gpsimd", lambda e, tc=tc: e.iota(
                scr1[0:16, 0:512], [[1, 512]], base=tc * 512, channel_multiplier=0,
                allow_small_or_imprecise_dtypes=True),
                waits=gi.waits(), inc=("gp", 1))
            gp_io = E.ct["gp"]
            E.op("scalar", lambda e, tc=tc: e.activation(
                decS[:, tc * 512:(tc + 1) * 512], scr1[0:16, 0:512], AF.Exp,
                scale=gamS[:, 1:2]),
                waits=[("gp", gp_io), ("sm", sm_lng)] + G("decS").waits(),
                inc=("sm", 1))
            gi.use("sm", E.ct["sm"])
        G("decS").reset()
        sm_dec = E.ct["sm"]
        E.barrier()

        if DEBUG_TAPS and s == 0:
            E.op("gpsimd", lambda e: e.dma_start(out=taps["tap_xnb"], in_=xnb2[:]),
                 waits=[("vm", vm_xnb2)], inc=("tp", 16))

        for half in range(2):
            jt0 = half * 4
            for tc in range(4):
                for jo in range(4):
                    jglob = jt0 + jo
                    wb = wsl1[jo % 2]
                    gw = G(f"wsl1{jo % 2}")
                    wl_k = wload(wb, "wk1T", jglob * 128, 128, waits=gw.waits())
                    pq = PS[jo % 2]; pg = psG[jo % 2]
                    for kt in range(8):
                        E.op("tensor", lambda e, kt=kt, tc=tc, wb=wb, pq=pq: e.matmul(
                            pq[:, 0:512], wb[:, kt * 128:(kt + 1) * 128],
                            xnb2[:, kt * T + tc * 512:kt * T + (tc + 1) * 512],
                            start=(kt == 0), stop=(kt == 7)),
                            waits=[("wl", wl_k), ("vm", vm_xnb2)] +
                            (pg.waits() if kt == 0 else []),
                            inc=(("tm", 1) if kt == 7 else None))
                    gw.use("tm", E.ct["tm"])
                    pg.reset()
                    tm_k = E.ct["tm"]
                    gk = G("K2b")
                    E.op("scalar", lambda e, jo=jo, tc=tc, jglob=jglob, pq=pq: e.activation(
                        K2b[:, jo * T + tc * 512:jo * T + (tc + 1) * 512], pq[:, 0:512],
                        AF.Identity, bias=sb["bk1S"][:, jglob:jglob + 1]),
                        waits=[("tm", tm_k)] + (gk.waits() if jo == 0 and tc == 0 else []),
                        inc=("sm", 1))
                    pg.use("sm", E.ct["sm"])
                wl_v = wload(wsl1[2], "wv1T", half * 512, 256, waits=G("wsl12").waits())
                wl_v2 = wload(wsl1[3], "wv1T", half * 512 + 256, 256, waits=G("wsl13").waits())
                for ts_ in range(4):
                    tsub = tc * 4 + ts_
                    for qr in range(2):
                        wb = wsl1[2 + qr]
                        wl_use = wl_v if qr == 0 else wl_v2
                        pq = PS[ts_ % 2]; pg = psG[ts_ % 2]
                        for kt in range(8):
                            E.op("tensor", lambda e, kt=kt, ts_=ts_, tc=tc, wb=wb, qr=qr, pq=pq: e.matmul(
                                pq[:, 0:256],
                                xnb2[:, kt * T + tc * 512 + ts_ * 128:kt * T + tc * 512 + (ts_ + 1) * 128],
                                wb[:, kt * 256:(kt + 1) * 256],
                                start=(kt == 0), stop=(kt == 7)),
                                waits=[("wl", wl_use), ("vm", vm_xnb2)] +
                                (pg.waits() if kt == 0 else []),
                                inc=(("tm", 1) if kt == 7 else None))
                        pg.reset()
                        tm_v = E.ct["tm"]
                        gv = G("V2b")
                        E.op("vector", lambda e, tsub=tsub, qr=qr, half=half, pq=pq: e.tensor_tensor(
                            V2b[:, tsub * 512 + qr * 256:tsub * 512 + (qr + 1) * 256],
                            pq[:, 0:256],
                            bvrS[:, half * 512 + qr * 256:half * 512 + (qr + 1) * 256],
                            ALU.add),
                            waits=[("tm", tm_v)] + (gv.waits() if tsub == 0 and qr == 0 else []),
                            inc=("vm", 1))
                        pg.use("vm", E.ct["vm"])
                G("wsl12").use("tm", E.ct["tm"])
                G("wsl13").use("tm", E.ct["tm"])
            G("K2b").reset(); G("V2b").reset()
            vm_V2 = E.ct["vm"]
            sm_K2 = E.ct["sm"]

            for qi in range(NQT):
                for jo in range(4):
                    jglob = jt0 + jo
                    wb = wsl1[jo % 2]
                    gw = G(f"wsl1{jo % 2}")
                    wl_q = wload(wb, "wq1T", jglob * 128, 128, waits=gw.waits())
                    for kt in range(8):
                        E.op("tensor", lambda e, kt=kt, qi=qi, wb=wb: e.matmul(
                            PS[3][:, 0:512], wb[:, kt * 128:(kt + 1) * 128],
                            xnb2[:, kt * T + qi * 512:kt * T + (qi + 1) * 512],
                            start=(kt == 0), stop=(kt == 7)),
                            waits=[("wl", wl_q), ("vm", vm_xnb2)] +
                            (psG[3].waits() if kt == 0 else []),
                            inc=(("tm", 1) if kt == 7 else None))
                    gw.use("tm", E.ct["tm"])
                    psG[3].reset()
                    tm_q = E.ct["tm"]
                    E.op("scalar", lambda e, jglob=jglob: e.activation(
                        scr1[:, 0:512], PS[3][:, 0:512], AF.Identity,
                        bias=sb["bq1S"][:, jglob:jglob + 1]),
                        waits=[("tm", tm_q)] + G("scr1").waits(), inc=("sm", 1))
                    sm_qr = E.ct["sm"]
                    psG[3].use("sm", sm_qr)
                    E.op("tensor", lambda e, jglob=jglob, qi=qi: e.matmul(
                        PS[2][:, 0:512], selS[:, jglob * 128:(jglob + 1) * 128],
                        decS[:, qi * 512:(qi + 1) * 512], start=True, stop=True),
                        waits=[("sm", sm_dec)] + psG[2].waits(), inc=("tm", 1))
                    tm_dr = E.ct["tm"]
                    psG[2].reset()
                    gq2 = G("Q2b")
                    E.op("vector", lambda e, jo=jo: e.tensor_tensor(
                        Q2b[:, jo * 512:(jo + 1) * 512], scr1[:, 0:512],
                        PS[2][:, 0:512], ALU.mult),
                        waits=[("sm", sm_qr), ("tm", tm_dr)] +
                        (gq2.waits() if jo == 0 else []), inc=("vm", 1))
                    G("scr1").use("vm", E.ct["vm"])
                    psG[2].use("vm", E.ct["vm"])
                vm_Q2 = E.ct["vm"]
                G("Q2b").reset()

                for jo in range(4):
                    nch_q = 4 * (qi + 1)

                    def emit_st1(cc):
                        par = cc % 2
                        pq = PS[par]; pg = psG[par]
                        diag = cc >= 4 * qi
                        o = (cc - 4 * qi) * 128 if diag else 0
                        E.op("tensor", lambda e, jo=jo, cc=cc, o=o, pq=pq: e.matmul(
                            pq[:, o:512],
                            K2b[0:64, jo * T + cc * 128:jo * T + (cc + 1) * 128],
                            Q2b[0:64, jo * 512 + o:(jo + 1) * 512],
                            start=True, stop=True),
                            waits=pg.waits() + [("vm", vm_Q2), ("sm", sm_K2)])
                        E.op("tensor", lambda e, jo=jo, cc=cc, o=o, pq=pq: e.matmul(
                            pq[:, 512 + o:1024],
                            K2b[64:128, jo * T + cc * 128:jo * T + (cc + 1) * 128],
                            Q2b[64:128, jo * 512 + o:(jo + 1) * 512],
                            start=True, stop=True, tile_position=(64, 0)),
                            inc=("tm", 1))
                        pg.reset()
                        return E.ct["tm"], o, diag, par

                    def emit_cpv1(cc, tm_s, o, diag, par, last):
                        pq = PS[par]; pg = psG[par]
                        gP = G(f"P1{par}")
                        if o == 0:
                            E.op("scalar", lambda e, par=par, pq=pq: e.activation(
                                P1b[:, par * 1024:(par + 1) * 1024], pq[:, 0:1024],
                                AF.Identity), waits=[("tm", tm_s)] + gP.waits(),
                                inc=("sm", 1))
                        else:
                            E.op("scalar", lambda e, par=par, pq=pq, o=o: e.activation(
                                P1b[:, par * 1024 + o:par * 1024 + 512], pq[:, o:512],
                                AF.Identity), waits=[("tm", tm_s)] + gP.waits(),
                                inc=("sm", 1))
                            E.op("scalar", lambda e, par=par, pq=pq, o=o: e.activation(
                                P1b[:, par * 1024 + 512 + o:par * 1024 + 1024],
                                pq[:, 512 + o:1024], AF.Identity), inc=("sm", 1))
                        sm_p = E.ct["sm"]
                        pg.use("sm", sm_p)
                        vmw = []
                        if diag:
                            E.op("vector", lambda e, par=par, o=o: e.tensor_tensor(
                                P1b[:, par * 1024 + o:par * 1024 + o + 128],
                                P1b[:, par * 1024 + o:par * 1024 + o + 128], triS[:],
                                ALU.mult), waits=[("sm", sm_p)], inc=("vm", 1))
                            E.op("vector", lambda e, par=par, o=o: e.tensor_tensor(
                                P1b[:, par * 1024 + 512 + o:par * 1024 + 512 + o + 128],
                                P1b[:, par * 1024 + 512 + o:par * 1024 + 512 + o + 128],
                                triS[:], ALU.mult), inc=("vm", 1))
                            vmw = [("vm", E.ct["vm"])]
                        pvw = [("sm", sm_p), ("vm", vm_V2)] + vmw
                        E.op("tensor", lambda e, cc=cc, jo=jo, par=par, o=o, last=last: e.matmul(
                            PS[2][0:64, o:512],
                            V2b[:, cc * 512 + (2 * jo) * 64:cc * 512 + (2 * jo) * 64 + 64],
                            P1b[:, par * 1024 + o:par * 1024 + 512],
                            start=(cc == 0), stop=last),
                            waits=pvw + (psG[2].waits() if cc == 0 else []))
                        E.op("tensor", lambda e, cc=cc, jo=jo, par=par, o=o, last=last: e.matmul(
                            PS[2][64:128, 512 + o:1024],
                            V2b[:, cc * 512 + (2 * jo + 1) * 64:cc * 512 + (2 * jo + 1) * 64 + 64],
                            P1b[:, par * 1024 + 512 + o:par * 1024 + 1024],
                            start=(cc == 0), stop=last, tile_position=(0, 64)),
                            inc=("tm", 1))
                        gP.use("tm", E.ct["tm"])
                        if cc == 0:
                            psG[2].reset()

                    prev = None
                    for cc in range(nch_q):
                        cur = (cc,) + emit_st1(cc)
                        if prev is not None:
                            emit_cpv1(prev[0], prev[1], prev[2], prev[3], prev[4],
                                      prev[0] == nch_q - 1)
                        prev = cur
                    emit_cpv1(prev[0], prev[1], prev[2], prev[3], prev[4], True)
                    tm_pv = E.ct["tm"]
                    gA = G("aO1")
                    E.op("scalar", lambda e, jo=jo: e.activation(
                        aO1[0:64, jo * 1024:jo * 1024 + 512],
                        PS[2][0:64, 0:512], AF.Identity),
                        waits=[("tm", tm_pv)] + (gA.waits() if jo == 0 else []),
                        inc=("sm", 1))
                    E.op("scalar", lambda e, jo=jo: e.activation(
                        aO1[64:128, jo * 1024:jo * 1024 + 512],
                        PS[2][64:128, 512:1024], AF.Identity), inc=("sm", 1))
                    psG[2].use("sm", E.ct["sm"])
                G("aO1").reset()
                sm_aO = E.ct["sm"]

                for dj in range(NT):
                    wb = wsl1[2 + dj % 2]
                    gw = G(f"wsl1{2 + dj % 2}")
                    wl_o = wload(wb, "wo1T", dj * 128, 128, waits=gw.waits())
                    for kk in range(4):
                        kt = jt0 + kk
                        E.op("tensor", lambda e, kk=kk, kt=kt, wb=wb: e.matmul(
                            PS[3][:, 0:512], wb[:, kt * 128:(kt + 1) * 128],
                            aO1[:, kk * 1024:kk * 1024 + 512],
                            start=(kk == 0), stop=(kk == 3)),
                            waits=[("wl", wl_o), ("sm", sm_aO)] +
                            (psG[3].waits() if kk == 0 else []),
                            inc=(("tm", 1) if kk == 3 else None))
                    gw.use("tm", E.ct["tm"])
                    psG[3].reset()
                    tm_wo = E.ct["tm"]
                    if half == 0:
                        E.op("vector", lambda e, dj=dj, qi=qi: e.scalar_tensor_tensor(
                            stF[:, dj * T + qi * 512:dj * T + (qi + 1) * 512],
                            PS[3][:, 0:512], sb["bo1S"][:, dj:dj + 1],
                            stF[:, dj * T + qi * 512:dj * T + (qi + 1) * 512],
                            ALU.add, ALU.add), waits=[("tm", tm_wo)], inc=("vm", 1))
                    else:
                        E.op("vector", lambda e, dj=dj, qi=qi: e.tensor_add(
                            stF[:, dj * T + qi * 512:dj * T + (qi + 1) * 512],
                            PS[3][:, 0:512],
                            stF[:, dj * T + qi * 512:dj * T + (qi + 1) * 512]),
                            waits=[("tm", tm_wo)], inc=("vm", 1))
                    psG[3].use("vm", E.ct["vm"])
                G("aO1").use("tm", E.ct["tm"])
            G("K2b").use("tm", E.ct["tm"])
            G("V2b").use("tm", E.ct["tm"])
        G("selS").use("tm", E.ct["tm"])
        E.barrier()

        wl_slot = {}

        def load_triple(hc):
            slot = hc % 9
            gw = G(f"ring{slot}")
            wload(ringG[slot], "wg1T", hc * 128, 128, waits=gw.waits())
            wload(ringU[slot], "wu1T", hc * 128, 128)
            E.op("gpsimd", lambda e, hc=hc, slot=slot: e.dma_start(
                out=ringD[slot][:, 0:1024], in_=RH["wd1T"][:, hc, 0:1024]),
                inc=("wl", 16))
            wl_slot[slot] = E.ct["wl"]
            gw.reset()

        for hc in range(6):
            load_triple(hc)
        for blk in range(4):
            if blk < 3:
                for hc in range(blk * 6 + 6, blk * 6 + 9):
                    load_triple(hc)
            for tt in range(NQT):
                hcs = [blk * 6 + i for i in range(6)]
                for i, hc in enumerate(hcs):
                    slot = hc % 9
                    pq = PS[i % 2]; pg = psG[i % 2]
                    for kt in range(8):
                        E.op("tensor", lambda e, kt=kt, slot=slot, tt=tt, pq=pq: e.matmul(
                            pq[:, 0:512], ringG[slot][:, kt * 128:(kt + 1) * 128],
                            xnb2[:, kt * T + tt * 512:kt * T + (tt + 1) * 512],
                            start=(kt == 0), stop=(kt == 7)),
                            waits=[("wl", wl_slot[slot]), ("vm", vm_xnb2)] +
                            (pg.waits() if kt == 0 else []),
                            inc=(("tm", 1) if kt == 7 else None))
                    pg.reset()
                    tm_g = E.ct["tm"]
                    gf_ = G("fscr")
                    E.op("scalar", lambda e, pq=pq: e.activation(
                        fscr[:, 0:512], pq[:, 0:512], AF.Silu, scale=S127),
                        waits=[("tm", tm_g)] + gf_.waits(), inc=("sm", 1))
                    sm_si = E.ct["sm"]
                    pg.use("sm", sm_si)
                    pq2 = PS[(i + 1) % 2]; pg2 = psG[(i + 1) % 2]
                    for kt in range(8):
                        E.op("tensor", lambda e, kt=kt, slot=slot, tt=tt, pq2=pq2: e.matmul(
                            pq2[:, 0:512], ringU[slot][:, kt * 128:(kt + 1) * 128],
                            xnb2[:, kt * T + tt * 512:kt * T + (tt + 1) * 512],
                            start=(kt == 0), stop=(kt == 7)),
                            waits=[("wl", wl_slot[slot])] + (pg2.waits() if kt == 0 else []),
                            inc=(("tm", 1) if kt == 7 else None))
                    pg2.reset()
                    tm_u = E.ct["tm"]
                    gh = G("hblk")
                    E.op("vector", lambda e, i=i, pq2=pq2: e.tensor_tensor(
                        hblk[:, i * 512:(i + 1) * 512], fscr[:, 0:512], pq2[:, 0:512],
                        ALU.mult),
                        waits=[("tm", tm_u), ("sm", sm_si)] +
                        (gh.waits() if i == 0 else []), inc=("vm", 1))
                    pg2.use("vm", E.ct["vm"])
                    G("fscr").use("vm", E.ct["vm"])
                G("hblk").reset()
                vm_h = E.ct["vm"]
                for dj in range(NT):
                    for i, hc in enumerate(hcs):
                        slot = hc % 9
                        E.op("tensor", lambda e, i=i, slot=slot, dj=dj: e.matmul(
                            PS[3][:, 0:512], ringD[slot][:, dj * 128:(dj + 1) * 128],
                            hblk[:, i * 512:(i + 1) * 512],
                            start=(i == 0), stop=(i == 5)),
                            waits=[("wl", wl_slot[slot]), ("vm", vm_h)] +
                            (psG[3].waits() if i == 0 else []),
                            inc=(("tm", 1) if i == 5 else None))
                    psG[3].reset()
                    tm_d = E.ct["tm"]
                    E.op("vector", lambda e, dj=dj, tt=tt: e.scalar_tensor_tensor(
                        stF[:, dj * T + tt * 512:dj * T + (tt + 1) * 512],
                        PS[3][:, 0:512], S127 * S127,
                        stF[:, dj * T + tt * 512:dj * T + (tt + 1) * 512],
                        ALU.mult, ALU.add), waits=[("tm", tm_d)], inc=("vm", 1))
                    psG[3].use("vm", E.ct["vm"])
                G("hblk").use("tm", E.ct["tm"])
            for hc in hcs:
                G(f"ring{hc % 9}").use("tm", E.ct["tm"])
            if blk < 3:
                for hc in range(blk * 6 + 9, blk * 6 + 12):
                    load_triple(hc)
        E.barrier()

        if DEBUG_TAPS and s == 0:
            for j in range(NT):
                E.op("sync", lambda e, j=j: e.dma_start(
                    out=taps["tap_state2"][j * 128:(j + 1) * 128, :],
                    in_=stF[:, j * T:(j + 1) * T]),
                    waits=[("vm", E.ct["vm"]), ("tm", E.ct["tm"])], inc=("st", 16))

        vm_fin, tm_fin = E.ct["vm"], E.ct["tm"]
        for j in range(NT):
            g = G(f"stF{j}")
            E.op("sync", lambda e, j=j: e.dma_start(
                out=out[s, j * 128:(j + 1) * 128, :], in_=stF[:, j * T:(j + 1) * T]),
                waits=[("vm", vm_fin), ("tm", tm_fin)], inc=("st", 16))
            g.use("st", E.ct["st"])

    E.waitall("sync", ("st",))
    if DEBUG_TAPS:
        E.waitall("gpsimd", ("tp",))



_NC_CACHE = {}


def kernel(**inputs):
    x = np.asarray(inputs["L0_attn_x"], dtype=np.float32)
    shared = stage_shared(inputs)
    in_maps = []
    for cidx in range(NCORES):
        m = dict(shared)
        m["xin"] = np.ascontiguousarray(x[2 * cidx:2 * cidx + 2].transpose(0, 2, 1))
        in_maps.append(m)
    if "nc" not in _NC_CACHE:
        _NC_CACHE["nc"] = build_nc()
    res = run_bass_kernel_spmd(_NC_CACHE["nc"], in_maps, core_ids=list(range(NCORES)))
    outs = [np.asarray(res.results[cidx]["out"]).transpose(0, 2, 1)
            for cidx in range(NCORES)]
    return np.concatenate(outs, axis=0).astype(np.float32)


if __name__ == "__main__":
    print("building program ...")
    build_nc()
    print("build ok")


# version 03
# speedup vs baseline: 1.0594x; 1.0594x over previous
import contextlib
import math

import numpy as np
import ml_dtypes

import concourse.bass as bass
import concourse.mybir as mybir
from concourse.bass_utils import run_bass_kernel_spmd

F32 = mybir.dt.float32
BF16 = mybir.dt.bfloat16
I32 = mybir.dt.int32
AF = mybir.ActivationFunctionType
ALU = mybir.AluOpType
AX = mybir.AxisListType

NCORES = 8
SEQS = 2
T = 2048
D = 1024
NT = 8
H = 16
DK = 64
TQ = 512
NQT = T // TQ
CH = 128
FH = 3072
S127 = 1.0 / 127.0
EPS = 1e-5

DEBUG_TAPS = False
DEBUG_PRINT = False



def _rope_perm():
    perm = np.zeros(D, dtype=np.int64)
    for r in range(512):
        h, p = r // 32, r % 32
        perm[r] = h * 64 + 2 * p
        perm[512 + r] = h * 64 + 2 * p + 1
    return perm


def _rope_tables():
    powers = np.arange(0, DK, 2, dtype=np.float32) / DK
    freqs = (1.0 / (10000.0 ** powers)).astype(np.float32)
    t = np.arange(T, dtype=np.float32)
    ang = np.outer(freqs, t).astype(np.float32)
    cosT = np.tile(np.cos(ang).astype(np.float32), (4, 1)).astype(ml_dtypes.bfloat16)
    sinT = np.tile(np.sin(ang).astype(np.float32), (4, 1)).astype(ml_dtypes.bfloat16)
    return cosT, sinT


def _col_tile(v, n):
    return np.ascontiguousarray(np.asarray(v, np.float32).reshape(n, 128).T)


def stage_shared(i):
    perm = _rope_perm()
    cosT, sinT = _rope_tables()
    f32 = lambda a: np.ascontiguousarray(np.asarray(a, dtype=np.float32))
    sel = np.zeros((H, NT * 128), dtype=np.float32)
    for j in range(NT):
        for p in range(128):
            sel[2 * j + (1 if p >= 64 else 0), j * 128 + p] = 1.0
    e4 = np.zeros((128, 128), dtype=np.float32)
    for p in range(128):
        e4[p, 32 * (p // 32)] = 1.0
    d = {
        "wq0T": f32(np.asarray(i["L0_attn_wq"])[perm].T),
        "wk0T": f32(np.asarray(i["L0_attn_wk"])[perm].T),
        "wv0T": f32(np.asarray(i["L0_attn_wv"]).T),
        "wo0T": f32(np.asarray(i["L0_attn_wo"]).T),
        "cn1": f32(i["L0_ffn_cn_w1"]), "cn2": f32(i["L0_ffn_cn_w2"]),
        "cn3": f32(i["L0_ffn_cn_w3"]), "cn4": f32(i["L0_ffn_cn_w4"]),
        "cn1b": _col_tile(i["L0_ffn_cn_b1"], 8),
        "cn2b": _col_tile(i["L0_ffn_cn_b2"], 4),
        "cn3b": _col_tile(i["L0_ffn_cn_b3"], 2),
        "cn4b": f32(i["L0_ffn_cn_b4"]).reshape(1, 1),
        "cn4c": _col_tile(i["L0_ffn_cn_w4"].reshape(-1), 2),
        "gc3c": _col_tile(i["L0_ffn_gc_w3"].reshape(-1), 2),
        "gc1": f32(i["L0_ffn_gc_w1"]), "gc2": f32(i["L0_ffn_gc_w2"]),
        "gc3": f32(i["L0_ffn_gc_w3"]),
        "gc1b": _col_tile(i["L0_ffn_gc_b1"], 4),
        "gc2b": _col_tile(i["L0_ffn_gc_b2"], 2),
        "gc3b": f32(i["L0_ffn_gc_b3"]).reshape(1, 1),
        "ar": f32(i["L0_ffn_adaptive_range"]).reshape(1, 1),
        "wq1T": f32(np.asarray(i["L1_attn_Wq"]).T),
        "wk1T": f32(np.asarray(i["L1_attn_Wk"]).T),
        "wv1T": f32(np.asarray(i["L1_attn_Wv"]).T),
        "wo1T": f32(np.asarray(i["L1_attn_Wo"]).T),
        "bq1": _col_tile(i["L1_attn_bq"], 8),
        "bk1": _col_tile(i["L1_attn_bk"], 8),
        "bv1r": f32(i["L1_attn_bv"]).reshape(1, D),
        "bo1": _col_tile(i["L1_attn_bo"], 8),
        "gam": f32(i["L1_attn_gamma"]).reshape(H, 1),
        "wg1T": np.ascontiguousarray(np.asarray(i["L1_ffn_w_gate"], np.int32).T),
        "wu1T": np.ascontiguousarray(np.asarray(i["L1_ffn_w_up"], np.int32).T),
        "wd1T": np.ascontiguousarray(np.asarray(i["L1_ffn_w_down"], np.int32).T),
        "cosT": cosT, "sinT": sinT,
        "tri": np.triu(np.ones((CH, CH), np.float32)).astype(ml_dtypes.bfloat16),
        "sel": sel.astype(ml_dtypes.bfloat16),
        "e4sel": e4.astype(ml_dtypes.bfloat16),
    }
    return d



class Em:
    def __init__(self, nc, sems):
        self.nc = nc
        self.sems = sems
        self.cur = None
        self.ct = {}
        self.lastwait = {}

    def start(self, engine):
        self.cur = engine
        self.ct = {k: 0 for k in self.sems}
        self.lastwait = {}

    def op(self, eng, fn, waits=(), inc=None):
        if inc is not None and inc[0] in ("ld", "wl", "st"):
            waits = list(waits) + [(inc[0], self.ct[inc[0]])]
        if eng == self.cur:
            e = getattr(self.nc, eng)
            for st, v in waits:
                if v > 0 and self.lastwait.get(st, -1) < v:
                    e.wait_ge(self.sems[st], v)
                    self.lastwait[st] = v
            ins = fn(e)
            if inc is not None and ins is not None:
                ins.then_inc(self.sems[inc[0]], inc[1])
        if inc is not None:
            self.ct[inc[0]] += inc[1]

    def waitall(self, eng, sems, extra=()):
        vals = [(st, self.ct[st]) for st in sems] + list(extra)
        if eng == self.cur:
            e = getattr(self.nc, eng)
            for st, v in vals:
                if v > 0 and self.lastwait.get(st, -1) < v:
                    e.wait_ge(self.sems[st], v)
                    self.lastwait[st] = v

    def vflush(self):
        self.op("vector", lambda e: e.engine_nop(), inc=("vm", 1))
        self.waitall("vector", ("vm",))

    def barrier(self):
        for eng in ("tensor", "scalar", "vector", "gpsimd", "sync"):
            self.waitall(eng, ("tm", "sm", "vm", "ld"))


class Guard:
    def __init__(self):
        self.w = {}

    def use(self, sem, ct):
        self.w[sem] = max(self.w.get(sem, 0), ct)

    def waits(self):
        return list(self.w.items())

    def reset(self):
        self.w = {}



def build_nc():
    nc = bass.Bass("TRN2", target_bir_lowering=False, detect_race_conditions=False,
                   debug=DEBUG_PRINT)
    dram = {}

    def din(name, shape, dt=F32):
        dram[name] = nc.dram_tensor(name, shape, dt, kind="ExternalInput").ap()

    din("xin", [SEQS, D, T])
    for n in ("wq0T", "wk0T", "wv0T", "wo0T", "wq1T", "wk1T", "wv1T", "wo1T"):
        din(n, [D, D])
    din("cn1", [D, D]); din("cn2", [D, 512]); din("cn3", [512, 256]); din("cn4", [256, 1])
    din("cn1b", [128, 8]); din("cn2b", [128, 4]); din("cn3b", [128, 2]); din("cn4b", [1, 1])
    din("gc1", [D, 512]); din("gc2", [512, 256]); din("gc3", [256, 1])
    din("cn4c", [128, 2]); din("gc3c", [128, 2])
    din("gc1b", [128, 4]); din("gc2b", [128, 2]); din("gc3b", [1, 1]); din("ar", [1, 1])
    din("bq1", [128, 8]); din("bk1", [128, 8]); din("bv1r", [1, D]); din("bo1", [128, 8])
    din("gam", [H, 1])
    din("wg1T", [D, FH], I32); din("wu1T", [D, FH], I32); din("wd1T", [FH, D], I32)
    din("cosT", [128, T], BF16); din("sinT", [128, T], BF16)
    din("tri", [CH, CH], BF16); din("sel", [H, NT * 128], BF16)
    din("e4sel", [128, 128], BF16)

    out = nc.dram_tensor("out", [SEQS, D, T], F32, kind="ExternalOutput").ap()

    taps = {}
    if DEBUG_TAPS:
        taps["tap_rows"] = nc.dram_tensor("tap_rows", [1, 4 * T], F32, kind="ExternalOutput").ap()
        taps["tap_K"] = nc.dram_tensor("tap_K", [128, NT * T], F32, kind="ExternalOutput").ap()
        taps["tap_V"] = nc.dram_tensor("tap_V", [128, 16 * 1040], F32, kind="ExternalOutput").ap()
        taps["tap_state"] = nc.dram_tensor("tap_state", [D, T], F32, kind="ExternalOutput").ap()
        taps["tap_xnb"] = nc.dram_tensor("tap_xnb", [128, NT * T], F32, kind="ExternalOutput").ap()
        taps["tap_state2"] = nc.dram_tensor("tap_state2", [D, T], F32, kind="ExternalOutput").ap()
        taps["tap_x1"] = nc.dram_tensor("tap_x1", [D, T], F32, kind="ExternalOutput").ap()
        taps["tap_dbg"] = nc.dram_tensor("tap_dbg", [128, 48], F32, kind="ExternalOutput").ap()

    RH = {}
    for n in ("wq0T", "wk0T", "wv0T", "wo0T", "cn1", "cn2", "wq1T", "wk1T",
              "wv1T", "wo1T", "wg1T", "wu1T", "cn3", "gc1", "gc2", "wd1T"):
        RH[n] = dram[n].rearrange("(kt p) n -> p kt n", p=128)
    RH64 = {"wo0T": dram["wo0T"].rearrange("(h p) n -> p h n", p=64)}
    xre_src = dram["xin"].rearrange("s (j p) t -> s p j t", p=128)

    sb = {}

    def fixed(name, shape, dt):
        sb[name] = nc.alloc_sbuf_tensor(name, shape, dt)

    fixed("stateF", [128, NT * T], F32)
    fixed("triS", [CH, CH], BF16)
    fixed("mrow", [1, T], F32)
    fixed("rrow", [1, T], F32)
    fixed("srow", [128, 1024], F32)
    fixed("onesb", [128, 8], BF16)
    fixed("onesf", [128, 128], F32)
    fixed("b1S", [128, 8], F32)
    fixed("b2S", [128, 4], F32)
    fixed("b3S", [128, 2], F32)
    fixed("gb1S", [128, 4], F32)
    fixed("gb2S", [128, 2], F32)
    fixed("bq1S", [128, 8], F32)
    fixed("bk1S", [128, 8], F32)
    fixed("bo1S", [128, 8], F32)
    fixed("smalls", [128, 16], F32)
    fixed("gamS", [H, 2], F32)
    fixed("kprS", [128, 64], F32)
    fixed("e4S", [128, 128], BF16)
    fixed("negb", [128, 128], BF16)
    fixed("gmS", [128, 16], F32)
    fixed("gmB", [128, 16], BF16)

    SB_TOP = 229376
    rem = nc.sbuf_bytes_remaining
    arena_base = (SB_TOP - rem + 31) & ~31
    arena_bytes = SB_TOP - arena_base

    def at(name, off, shape, dt):
        nbytes = shape[1] * {F32: 4, BF16: 2, I32: 4}[dt]
        assert off + nbytes <= arena_bytes, (name, off, nbytes, arena_bytes)
        sb[name] = nc.alloc_sbuf_tensor_at(name, shape, dt, offset=arena_base + off)

    at("Kb", 0, [128, NT * T], BF16)
    at("Vb", 32768, [128, 16 * 1040], BF16)
    at("xnq", 66048, [128, NT * TQ], BF16)
    at("Qb", 74240, [128, NT * TQ], BF16)
    at("wbig", 74240, [128, 4096], BF16)
    at("attnO", 82432, [64, H * TQ], BF16)
    at("qraw", 82432, [128, NT * TQ], BF16)
    at("Pb", 98816, [128, 2048], BF16)
    at("oscr", 102912, [64, 512], BF16)
    at("wslA", 103936, [128, 1024], BF16)
    at("wslB", 105984, [128, 1024], BF16)
    at("wo64A", 108032, [64, 2048], BF16)
    at("c8b", 108032, [128, 2048], BF16)
    at("scrC", 112128, [128, 512], F32)
    at("sqb", 114176, [128, 1024], BF16)
    at("cosS", 116224, [128, T], BF16)
    at("sinS", 120320, [128, T], BF16)
    endC = 124416

    at("xre", 0, [128, NT * TQ], F32)
    at("stT", 16384, [128, NT * TQ], F32)
    at("outb", 32768, [128, NT * TQ], BF16)
    at("h1b", 40960, [128, NT * TQ], BF16)
    at("h2b", 49152, [128, 4 * TQ], BF16)
    at("h3b", 53248, [128, 2 * TQ], BF16)
    at("cn1S", 55296, [128, 8 * 1024], BF16)
    at("cn2S", 71680, [128, 8 * 512], BF16)
    at("cn3S", 79872, [128, 4 * 256], BF16)
    at("cn4S", 81920, [128, 2], BF16)
    at("twsc", 81952, [128, 2 * TQ], F32)
    at("gcsS", 86048, [128, 1024], BF16)
    endD = 88096

    at("xnb2", 0, [128, NT * T], BF16)
    at("K2b", 32768, [128, 4 * T], BF16)
    at("V2b", 49152, [128, 16 * TQ], BF16)
    at("Q2b", 65536, [128, 4 * TQ], BF16)
    at("aO1", 69632, [128, NT * TQ], BF16)
    at("P1b", 77824, [128, 2048], BF16)
    at("scr1", 81920, [128, 512], F32)
    at("wsl1A", 83968, [128, 1024], BF16)
    at("wsl1B", 86016, [128, 1024], BF16)
    at("wsl1C", 88064, [128, 2048], BF16)
    at("wsl1D", 92160, [128, 2048], BF16)
    at("sqb1", 96256, [128, 1024], BF16)
    at("decS", 98304, [H, T], BF16)
    at("selS", 102400, [H, NT * 128], BF16)
    at("bvrow", 104448, [1, D], F32)
    at("bvrS", 108544, [128, D], BF16)
    endE = 110592

    NRING = 9
    for r in range(NRING):
        at(f"rg{r}", 32768 + r * 6144, [128, 1024], BF16)
        at(f"ru{r}", 32768 + r * 6144 + 2048, [128, 1024], BF16)
        at(f"rd{r}", 32768 + r * 6144 + 4096, [128, 1024], BF16)
    at("hblk", 32768 + NRING * 6144, [128, 6 * TQ], BF16)
    at("fscr", 32768 + NRING * 6144 + 6144, [128, TQ], BF16)
    endF = 32768 + NRING * 6144 + 6144 + 1024

    for e in (endC, endD, endE, endF):
        assert e <= arena_bytes, (e, arena_bytes)

    PS = [nc.alloc_psum_tensor(f"PS{i}", [128, 1024], F32) for i in range(4)]

    es = contextlib.ExitStack()
    sems = {n: es.enter_context(nc.semaphore(n))
            for n in ("ld", "wl", "tm", "sm", "vm", "st", "tp", "gp")}
    E = Em(nc, sems)
    ctx = dict(dram=dram, RH=RH, RH64=RH64, sb=sb, PS=PS, out=out, taps=taps,
               xre_src=xre_src)

    block = es.enter_context(nc.Block())

    def runner(name):
        def body(_):
            E.start(name)
            program(E, ctx)
        return body

    block.tensor(runner("tensor"))
    block.scalar(runner("scalar"))
    block.vector(runner("vector"))
    block.gpsimd(runner("gpsimd"))
    block.sync(runner("sync"))
    es.close()
    return nc



def program(E, c):
    dram, RH, RH64, sb, PS = c["dram"], c["RH"], c["RH64"], c["sb"], c["PS"]
    out, taps, xre_src = c["out"], c["taps"], c["xre_src"]

    stF = sb["stateF"]; Kb = sb["Kb"]; Vb = sb["Vb"]; wbig = sb["wbig"]
    xnq = sb["xnq"]; Qb = sb["Qb"]; attnO = sb["attnO"]; qraw = sb["qraw"]
    Pb = sb["Pb"]; oscr = sb["oscr"]; scrC = sb["scrC"]; sqb = sb["sqb"]
    wslA = sb["wslA"]; wslB = sb["wslB"]; wo64A = sb["wo64A"]
    mrow = sb["mrow"]; rrow = sb["rrow"]; srow = sb["srow"]
    cosS = sb["cosS"]; sinS = sb["sinS"]; triS = sb["triS"]
    onesb = sb["onesb"]; onesf = sb["onesf"]
    smalls = sb["smalls"]; gamS = sb["gamS"]
    kprS = sb["kprS"]; e4S = sb["e4S"]; negb = sb["negb"]
    gmS = sb["gmS"]; gmB = sb["gmB"]; c8b = sb["c8b"]
    selS = sb["selS"]
    decS = sb["decS"]; bvrS = sb["bvrS"]; bvrow = sb["bvrow"]
    xre = sb["xre"]; stT = sb["stT"]; outb = sb["outb"]
    h1b = sb["h1b"]; h2b = sb["h2b"]; h3b = sb["h3b"]
    cn1S = sb["cn1S"]; cn2S = sb["cn2S"]; cn3S = sb["cn3S"]; cn4S = sb["cn4S"]
    twsc = sb["twsc"]; gcsS = sb["gcsS"]
    xnb2 = sb["xnb2"]; K2b = sb["K2b"]; V2b = sb["V2b"]; Q2b = sb["Q2b"]
    aO1 = sb["aO1"]; P1b = sb["P1b"]; scr1 = sb["scr1"]; sqb1 = sb["sqb1"]
    wsl1 = [sb["wsl1A"], sb["wsl1B"], sb["wsl1C"], sb["wsl1D"]]
    ringG = [sb[f"rg{r}"] for r in range(9)]
    ringU = [sb[f"ru{r}"] for r in range(9)]
    ringD = [sb[f"rd{r}"] for r in range(9)]
    hblk = sb["hblk"]; fscr = sb["fscr"]

    psG = [Guard() for _ in range(4)]
    bufG = {}

    def G(name):
        if name not in bufG:
            bufG[name] = Guard()
        return bufG[name]

    def dma(dst, src):
        E.op("sync", lambda e, d=dst, s=src: e.dma_start(out=d, in_=s), inc=("ld", 16))

    dma(triS[:], dram["tri"])
    dma(sb["b1S"][:], dram["cn1b"]); dma(sb["b2S"][:], dram["cn2b"])
    dma(sb["b3S"][:], dram["cn3b"])
    dma(sb["gb1S"][:], dram["gc1b"]); dma(sb["gb2S"][:], dram["gc2b"])
    dma(sb["bq1S"][:], dram["bq1"]); dma(sb["bk1S"][:], dram["bk1"])
    dma(sb["bo1S"][:], dram["bo1"])
    dma(e4S[:], dram["e4sel"])
    dma(smalls[0:1, 0:1], dram["cn4b"]); dma(smalls[0:1, 1:2], dram["gc3b"])
    dma(smalls[0:1, 2:3], dram["ar"]); dma(gamS[:, 0:1], dram["gam"])
    ld_setup = E.ct["ld"]

    E.op("vector", lambda e: e.memset(onesb[:], 1.0), inc=("vm", 1))
    E.op("vector", lambda e: e.memset(onesf[:], 1.0), inc=("vm", 1))

    E.op("vector", lambda e: e.memset(smalls[0:1, 14:15], EPS), inc=("vm", 1))
    E.op("vector", lambda e: e.memset(negb[:], -1.0), inc=("vm", 1))
    vm_ones = E.ct["vm"]

    E.op("scalar", lambda e: e.activation(gamS[:, 1:2], gamS[:, 0:1], AF.Ln),
         waits=[("ld", ld_setup)], inc=("sm", 1))
    sm_lng = E.ct["sm"]

    def wload(buf, rhname, col0, ncols, kt=8, waits=()):
        src = RH[rhname][:, 0:kt, col0:col0 + ncols]
        dst = buf[:, 0:kt * ncols].rearrange("p (k n) -> p k n", n=ncols)
        E.op("gpsimd", lambda e, d=dst, s=src: e.dma_start(out=d, in_=s),
             waits=waits, inc=("wl", 16))
        return E.ct["wl"]

    for s in range(SEQS):
        g = G("cosS")
        E.op("sync", lambda e: e.dma_start(out=cosS[:], in_=dram["cosT"]),
             waits=g.waits(), inc=("ld", 16))
        g.reset()
        g = G("sinS")
        E.op("sync", lambda e: e.dma_start(out=sinS[:], in_=dram["sinT"]),
             waits=g.waits(), inc=("ld", 16))
        g.reset()
        for j in range(NT):
            g = G(f"stF{j}")
            E.op("sync", lambda e, j=j: e.dma_start(
                out=stF[:, j * T:(j + 1) * T],
                in_=dram["xin"][s, j * 128:(j + 1) * 128, :]),
                waits=g.waits(), inc=("ld", 16))
            g.reset()
        ld_x = E.ct["ld"]

        def ln_stats(buf, gname):
            for ch in range(4):
                for j in range(NT):
                    gsq = G(gname)
                    E.op("scalar", lambda e, j=j, ch=ch: e.activation(
                        buf[:, 0:512], stF[:, j * T + ch * 512:j * T + (ch + 1) * 512],
                        AF.Identity), waits=[("ld", ld_x)] + gsq.waits(), inc=("sm", 1))
                    E.op("scalar", lambda e, j=j, ch=ch: e.activation(
                        buf[:, 512:1024], stF[:, j * T + ch * 512:j * T + (ch + 1) * 512],
                        AF.Square), inc=("sm", 1))
                    sm_cast = E.ct["sm"]
                    E.op("tensor", lambda e, j=j: e.matmul(
                        PS[3][0:1, 0:512], onesb[:, 0:1], buf[:, 0:512],
                        start=(j == 0), stop=(j == NT - 1)),
                        waits=[("sm", sm_cast), ("vm", vm_ones)] +
                        (psG[3].waits() if j == 0 else []), inc=("tm", 1))
                    E.op("tensor", lambda e, j=j: e.matmul(
                        PS[3][0:1, 512:1024], onesb[:, 0:1], buf[:, 512:1024],
                        start=(j == 0), stop=(j == NT - 1)), inc=("tm", 1))
                    gsq.use("tm", E.ct["tm"])
                psG[3].reset()
                tm_st = E.ct["tm"]
                E.op("vector", lambda e, ch=ch: e.tensor_scalar_mul(
                    mrow[0:1, ch * 512:(ch + 1) * 512], PS[3][0:1, 0:512], 1.0 / D),
                    waits=[("tm", tm_st)], inc=("vm", 1))
                E.op("vector", lambda e, ch=ch: e.tensor_scalar_mul(
                    srow[0:1, 0:512], PS[3][0:1, 512:1024], 1.0 / D), inc=("vm", 1))
                E.op("vector", lambda e, ch=ch: e.tensor_mul(
                    srow[0:1, 512:1024], mrow[0:1, ch * 512:(ch + 1) * 512],
                    mrow[0:1, ch * 512:(ch + 1) * 512]), inc=("vm", 1))
                E.op("vector", lambda e: e.tensor_sub(
                    srow[0:1, 0:512], srow[0:1, 0:512], srow[0:1, 512:1024]),
                    inc=("vm", 1))
                vm_var = E.ct["vm"]
                psG[3].use("vm", vm_var)
                E.op("scalar", lambda e: e.activation(
                    srow[0:1, 0:512], srow[0:1, 0:512], AF.Sqrt,
                    bias=smalls[0:1, 14:15]),
                    waits=[("vm", vm_var), ("vm", vm_ones)], inc=("sm", 1))
                sm_sqrt = E.ct["sm"]
                E.op("vector", lambda e, ch=ch: e.reciprocal(
                    rrow[0:1, ch * 512:(ch + 1) * 512], srow[0:1, 0:512]),
                    waits=[("sm", sm_sqrt)])
                E.op("vector", lambda e: e.engine_nop(), inc=("vm", 1))
                E.waitall("vector", ("vm",))

        ln_stats(sqb, "sqb")
        vm_rows = E.ct["vm"]
        E.barrier()

        if DEBUG_TAPS and s == 0:
            E.op("sync", lambda e: e.dma_start(out=taps["tap_rows"][0:1, 0:T], in_=mrow[:]),
                 waits=[("vm", vm_rows)], inc=("st", 16))
            E.op("sync", lambda e: e.dma_start(out=taps["tap_rows"][0:1, T:2 * T], in_=rrow[:]),
                 inc=("st", 16))

        def reps(cols0, extra_waits=(), fp32=True):
            E.op("tensor", lambda e: e.matmul(
                PS[3][:, 0:512], onesf[0:1, :], mrow[0:1, cols0:cols0 + 512],
                start=True, stop=True),
                waits=[("vm", vm_rows)] + psG[3].waits() + list(extra_waits),
                inc=("tm", 1))
            E.op("tensor", lambda e: e.matmul(
                PS[3][:, 512:1024], onesf[0:1, :], rrow[0:1, cols0:cols0 + 512],
                start=True, stop=True), inc=("tm", 1))
            psG[3].reset()
            return E.ct["tm"]

        def derive_xn(dstbuf, dst_off, cols0, tm_rep, gname):
            for j in range(NT):
                gx = G(gname)
                E.op("vector", lambda e, j=j, cols0=cols0: e.tensor_sub(
                    scrC[:, 0:512], stF[:, j * T + cols0:j * T + cols0 + 512],
                    PS[3][:, 0:512]),
                    waits=[("tm", tm_rep)] + G("scrC").waits() + (gx.waits() if j == 0 else []),
                    inc=("vm", 1))
                E.op("vector", lambda e, j=j: e.tensor_tensor(
                    dstbuf[:, dst_off + j * 512:dst_off + (j + 1) * 512],
                    scrC[:, 0:512], PS[3][:, 512:1024], ALU.mult), inc=("vm", 1))
            G(gname).reset()
            psG[3].use("vm", E.ct["vm"])
            return E.ct["vm"]

        E.op("vector", lambda e: e.memset(gmS[:, 0:8], 0.0), inc=("vm", 1))
        for tt in range(16):
            E.op("vector", lambda e, tt=tt: e.memset(
                Vb[:, tt * 1040:(tt + 1) * 1040].rearrange(
                    "p (h c) -> p h c", c=65)[:, :, 64:65], 1.0), inc=("vm", 1))
        for tc in range(4):
            tm_rep = reps(tc * 512)
            vm_xnq = derive_xn(xnq, 0, tc * 512, tm_rep, "xnq")
            for j in range(NT):
                E.op("vector", lambda e, j=j: e.reduce_sum(
                    smalls[:, 8:9], xnq[:, j * 512:(j + 1) * 512], axis=AX.X))
                E.op("vector", lambda e: e.engine_nop(), inc=("vm", 1))
                E.waitall("vector", ("vm",))
                E.op("vector", lambda e, j=j: e.tensor_add(
                    gmS[:, j:j + 1], gmS[:, j:j + 1], smalls[:, 8:9]), inc=("vm", 1))
            for jo in range(NT):
                wb = wslA if jo % 2 == 0 else wslB
                gw = G("wslA" if jo % 2 == 0 else "wslB")
                wl_k = wload(wb, "wk0T", jo * 128, 128, waits=gw.waits())
                pq = PS[0] if jo % 2 == 0 else PS[1]
                pg = psG[0] if jo % 2 == 0 else psG[1]
                for kt in range(8):
                    E.op("tensor", lambda e, kt=kt, wb=wb, pq=pq: e.matmul(
                        pq[:, 0:512], wb[:, kt * 128:(kt + 1) * 128],
                        xnq[:, kt * 512:(kt + 1) * 512],
                        start=(kt == 0), stop=(kt == 7)),
                        waits=[("wl", wl_k), ("vm", vm_xnq)] +
                        (pg.waits() if kt == 0 else []),
                        inc=(("tm", 1) if kt == 7 else None))
                gw.use("tm", E.ct["tm"])
                pg.reset()
                tm_k = E.ct["tm"]
                gq = G("attnO")
                E.op("scalar", lambda e, jo=jo, pq=pq: e.activation(
                    qraw[:, jo * 512:(jo + 1) * 512], pq[:, 0:512], AF.Identity),
                    waits=[("tm", tm_k)] + (gq.waits() if jo == 0 else []),
                    inc=("sm", 1))
                pg.use("sm", E.ct["sm"])
            G("attnO").reset()
            sm_kraw = E.ct["sm"]

            for je in range(4):
                jo2 = je + 4
                gs = G("sqb")
                E.op("vector", lambda e, je=je, tc=tc: e.tensor_tensor(
                    sqb[:, 0:512], qraw[:, je * 512:(je + 1) * 512],
                    cosS[:, tc * 512:(tc + 1) * 512], ALU.mult),
                    waits=[("sm", sm_kraw)] + gs.waits() + G("Kb").waits(),
                    inc=("vm", 1))
                E.op("vector", lambda e, jo2=jo2, tc=tc: e.tensor_tensor(
                    sqb[:, 512:1024], qraw[:, jo2 * 512:(jo2 + 1) * 512],
                    sinS[:, tc * 512:(tc + 1) * 512], ALU.mult), inc=("vm", 1))
                E.op("vector", lambda e, je=je, tc=tc: e.tensor_sub(
                    Kb[:, je * T + tc * 512:je * T + (tc + 1) * 512],
                    sqb[:, 0:512], sqb[:, 512:1024]), inc=("vm", 1))
                E.op("vector", lambda e, je=je, tc=tc: e.tensor_tensor(
                    sqb[:, 0:512], qraw[:, je * 512:(je + 1) * 512],
                    sinS[:, tc * 512:(tc + 1) * 512], ALU.mult), inc=("vm", 1))
                E.op("vector", lambda e, jo2=jo2, tc=tc: e.tensor_tensor(
                    sqb[:, 512:1024], qraw[:, jo2 * 512:(jo2 + 1) * 512],
                    cosS[:, tc * 512:(tc + 1) * 512], ALU.mult), inc=("vm", 1))
                E.op("vector", lambda e, jo2=jo2, tc=tc: e.tensor_add(
                    Kb[:, jo2 * T + tc * 512:jo2 * T + (tc + 1) * 512],
                    sqb[:, 0:512], sqb[:, 512:1024]), inc=("vm", 1))
                gs.use("vm", E.ct["vm"])
            G("attnO").use("vm", E.ct["vm"])

            for half in range(2):
                gw = G("wbig")
                wl_v = wload(wbig, "wv0T", half * 512, 512, waits=gw.waits())
                for ts_ in range(4):
                    tsub = tc * 4 + ts_
                    pq = PS[0] if ts_ % 2 == 0 else PS[1]
                    pg = psG[0] if ts_ % 2 == 0 else psG[1]
                    for kt in range(8):
                        E.op("tensor", lambda e, kt=kt, ts_=ts_, pq=pq: e.matmul(
                            pq[:, 0:512],
                            xnq[:, kt * 512 + ts_ * 128:kt * 512 + (ts_ + 1) * 128],
                            wbig[:, kt * 512:(kt + 1) * 512],
                            start=(kt == 0), stop=(kt == 7)),
                            waits=[("wl", wl_v), ("vm", vm_xnq)] +
                            (pg.waits() if kt == 0 else []),
                            inc=(("tm", 1) if kt == 7 else None))
                    pg.reset()
                    tm_v = E.ct["tm"]
                    dst = Vb[:, tsub * 1040:(tsub + 1) * 1040].rearrange(
                        "p (h c) -> p h c", c=65)[:, half * 8:(half + 1) * 8, 0:64]
                    E.op("scalar", lambda e, pq=pq, dst=dst: e.activation(
                        dst, pq[:, 0:512].rearrange("p (h c) -> p h c", c=64),
                        AF.Identity), waits=[("tm", tm_v)], inc=("sm", 1))
                    pg.use("sm", E.ct["sm"])
                gw.use("tm", E.ct["tm"])
        vm_endB = E.ct["vm"]

        for jq in range(4):
            for tc4 in range(4):
                for w2, jt in enumerate((jq, jq + 4)):
                    gs = G("sqb")
                    E.op("scalar", lambda e, jt=jt, tc4=tc4: e.activation(
                        sqb[:, 0:512], Kb[:, jt * T + tc4 * 512:jt * T + (tc4 + 1) * 512],
                        AF.Square), waits=[("vm", vm_endB), ("ld", ld_setup)] + gs.waits(),
                        inc=("sm", 1))
                    sm_sq = E.ct["sm"]
                    E.op("tensor", lambda e, w2=w2: e.matmul(
                        PS[3][:, 0:512], e4S[:], sqb[:, 0:512],
                        start=(w2 == 0), stop=(w2 == 1)),
                        waits=[("sm", sm_sq)] + (psG[3].waits() if w2 == 0 else []),
                        inc=("tm", 1))
                    gs.use("tm", E.ct["tm"])
                psG[3].reset()
                tm_kn = E.ct["tm"]
                for c4 in range(4):
                    E.op("vector", lambda e, jq=jq, tc4=tc4, c4=c4: e.reduce_max(
                        kprS[:, jq * 16 + tc4 * 4 + c4:jq * 16 + tc4 * 4 + c4 + 1],
                        PS[3][:, c4 * 128:(c4 + 1) * 128], axis=AX.X),
                        waits=[("tm", tm_kn)])
                E.op("vector", lambda e: e.engine_nop(), inc=("vm", 1))
                E.waitall("vector", ("vm",))
                psG[3].use("vm", E.ct["vm"])
            for sh in (1, 2, 4, 8):
                E.vflush()
                E.op("vector", lambda e, jq=jq, sh=sh: e.tensor_max(
                    kprS[:, jq * 16 + sh:jq * 16 + 16],
                    kprS[:, jq * 16 + sh:jq * 16 + 16],
                    kprS[:, jq * 16:jq * 16 + 16 - sh]), inc=("vm", 1))
        vm_kpr0 = E.ct["vm"]
        E.op("scalar", lambda e: e.activation(kprS[:], kprS[:], AF.Sqrt),
             waits=[("vm", vm_kpr0)], inc=("sm", 1))
        sm_kpr = E.ct["sm"]

        E.op("vector", lambda e: e.tensor_scalar_mul(gmS[:, 0:8], gmS[:, 0:8], 1.0 / T),
             waits=[("vm", vm_endB)], inc=("vm", 1))
        E.vflush()
        E.op("vector", lambda e: e.tensor_copy(gmB[:, 0:8], gmS[:, 0:8]), inc=("vm", 1))
        E.vflush()
        vm_gm = E.ct["vm"]
        for dj in range(4):
            gw = G("gcs")
            wl_g = wload(gcsS, "gc1", dj * 128, 128, waits=gw.waits())
            for kt in range(8):
                E.op("tensor", lambda e, kt=kt: e.matmul(
                    PS[3][:, 0:1], gcsS[:, kt * 128:(kt + 1) * 128], gmB[:, kt:kt + 1],
                    start=(kt == 0), stop=(kt == 7)),
                    waits=[("wl", wl_g), ("vm", vm_gm)] + (psG[3].waits() if kt == 0 else []),
                    inc=(("tm", 1) if kt == 7 else None))
            gw.use("tm", E.ct["tm"])
            psG[3].reset()
            E.op("scalar", lambda e, dj=dj: e.activation(
                gmB[:, 8 + dj:9 + dj], PS[3][:, 0:1], AF.Gelu,
                bias=sb["gb1S"][:, dj:dj + 1]), waits=[("tm", E.ct["tm"])], inc=("sm", 1))
            psG[3].use("sm", E.ct["sm"])
        sm_g1 = E.ct["sm"]
        for dj in range(2):
            gw = G("gcs")
            wl_g = wload(gcsS, "gc2", dj * 128, 128, kt=4, waits=gw.waits())
            for kt in range(4):
                E.op("tensor", lambda e, kt=kt: e.matmul(
                    PS[3][:, 0:1], gcsS[:, kt * 128:(kt + 1) * 128],
                    gmB[:, 8 + kt:9 + kt], start=(kt == 0), stop=(kt == 3)),
                    waits=[("wl", wl_g), ("sm", sm_g1)] + (psG[3].waits() if kt == 0 else []),
                    inc=(("tm", 1) if kt == 3 else None))
            gw.use("tm", E.ct["tm"])
            psG[3].reset()
            E.op("scalar", lambda e, dj=dj: e.activation(
                gmB[:, 12 + dj:13 + dj], PS[3][:, 0:1], AF.Gelu,
                bias=sb["gb2S"][:, dj:dj + 1]), waits=[("tm", E.ct["tm"])], inc=("sm", 1))
            psG[3].use("sm", E.ct["sm"])
        sm_g2 = E.ct["sm"]
        gw = G("gcs")
        E.op("gpsimd", lambda e: e.dma_start(out=gcsS[:, 0:2], in_=dram["gc3c"]),
             waits=gw.waits(), inc=("wl", 16))
        wl_g = E.ct["wl"]
        for kt in range(2):
            E.op("tensor", lambda e, kt=kt: e.matmul(
                PS[3][0:1, 0:1], gcsS[:, kt:kt + 1], gmB[:, 12 + kt:13 + kt],
                start=(kt == 0), stop=(kt == 1)),
                waits=[("wl", wl_g), ("sm", sm_g2)] + (psG[3].waits() if kt == 0 else []),
                inc=(("tm", 1) if kt == 1 else None))
        gw.use("tm", E.ct["tm"])
        psG[3].reset()
        E.op("scalar", lambda e: e.activation(
            smalls[0:1, 3:4], PS[3][0:1, 0:1], AF.Sigmoid, bias=smalls[0:1, 1:2]),
            waits=[("tm", E.ct["tm"]), ("ld", ld_setup)], inc=("sm", 1))
        psG[3].use("sm", E.ct["sm"])
        sm_gf = E.ct["sm"]
        E.op("vector", lambda e: e.tensor_mul(
            smalls[0:1, 6:7], smalls[0:1, 3:4], smalls[0:1, 2:3]),
            waits=[("sm", sm_gf)], inc=("vm", 1))
        E.op("vector", lambda e: e.tensor_scalar(
            smalls[0:1, 7:8], smalls[0:1, 2:3], -1.0, 0.1, ALU.mult, ALU.add),
            inc=("vm", 1))
        E.vflush()
        E.op("vector", lambda e: e.tensor_scalar(
            smalls[0:1, 4:5], smalls[0:1, 6:7], 1.4, smalls[0:1, 7:8],
            ALU.mult, ALU.add), inc=("vm", 1))
        E.op("vector", lambda e: e.tensor_scalar_mul(
            smalls[0:1, 5:6], smalls[0:1, 2:3], 0.6), inc=("vm", 1))
        E.vflush()
        vm_ab = E.ct["vm"]
        if DEBUG_TAPS and s == 0:
            E.op("sync", lambda e: e.dma_start(
                out=taps["tap_dbg"][:, 0:8], in_=gmS[:, 0:8]),
                waits=[("vm", vm_ab), ("sm", E.ct["sm"])], inc=("st", 16))
            E.op("gpsimd", lambda e: e.dma_start(
                out=taps["tap_dbg"][:, 16:30], in_=gmB[:, 0:14]), inc=("tp", 16))

        E.barrier()

        if DEBUG_TAPS and s == 0:
            E.op("gpsimd", lambda e: e.dma_start(out=taps["tap_K"], in_=Kb[:]),
                 waits=[("vm", E.ct["vm"]), ("sm", E.ct["sm"])], inc=("tp", 16))
            E.op("gpsimd", lambda e: e.dma_start(out=taps["tap_V"], in_=Vb[:]),
                 inc=("tp", 16))

        for qi in range(NQT):
            tm_rep = reps(qi * 512)
            vm_xq = derive_xn(xnq, 0, qi * 512, tm_rep, "xnq")
            for je in range(4):
                for sl, jj in enumerate((je, je + 4)):
                    wb = wslA if sl == 0 else wslB
                    gw = G("wslA" if sl == 0 else "wslB")
                    wl_q = wload(wb, "wq0T", jj * 128, 128, waits=gw.waits())
                    for kt in range(8):
                        E.op("tensor", lambda e, kt=kt, wb=wb: e.matmul(
                            PS[3][:, 0:512], wb[:, kt * 128:(kt + 1) * 128],
                            xnq[:, kt * 512:(kt + 1) * 512],
                            start=(kt == 0), stop=(kt == 7)),
                            waits=[("wl", wl_q), ("vm", vm_xq)] +
                            (psG[3].waits() if kt == 0 else []),
                            inc=(("tm", 1) if kt == 7 else None))
                    gw.use("tm", E.ct["tm"])
                    psG[3].reset()
                    tm_q = E.ct["tm"]
                    gq = G("attnO")
                    E.op("scalar", lambda e, sl=sl: e.activation(
                        qraw[:, sl * 512:(sl + 1) * 512], PS[3][:, 0:512], AF.Identity),
                        waits=[("tm", tm_q)] + gq.waits(), inc=("sm", 1))
                    psG[3].use("sm", E.ct["sm"])
                sm_qraw = E.ct["sm"]
                if DEBUG_PRINT and s == 0 and qi == 0 and je == 0:
                    E.op("vector", lambda e: e.print("xnq", xnq[0:1, 0:4]),
                         waits=[("sm", sm_qraw)])
                    E.op("vector", lambda e: e.print("qraw0", qraw[0:1, 0:4]))
                    E.op("vector", lambda e: e.print("qraw1", qraw[0:1, 512:516]))
                    E.op("vector", lambda e: e.print("ps3a", PS[3][0:1, 0:4]))
                    E.op("vector", lambda e: e.print("ps3b", PS[3][0:1, 512:516]))
                    E.op("vector", lambda e: e.print("qrawP", qraw[0:4, 0:1],
                                                     qraw[0:4, 513:514]))
                    E.op("vector", lambda e: e.print("xnqT", xnq[0:1, 512:514],
                                                     xnq[0:1, 1024:1026],
                                                     xnq[0:1, 3584:3586]))
                gs = G("sqb")
                gQ = G("Qb")
                E.op("vector", lambda e, qi=qi: e.tensor_tensor(
                    sqb[:, 0:512], qraw[:, 0:512], cosS[:, qi * 512:(qi + 1) * 512],
                    ALU.mult), waits=[("sm", sm_qraw)] + gs.waits() + gQ.waits(),
                    inc=("vm", 1))
                E.op("vector", lambda e, qi=qi: e.tensor_tensor(
                    sqb[:, 512:1024], qraw[:, 512:1024],
                    sinS[:, qi * 512:(qi + 1) * 512], ALU.mult), inc=("vm", 1))
                E.op("vector", lambda e, je=je: e.tensor_sub(
                    Qb[:, je * 512:(je + 1) * 512], sqb[:, 0:512], sqb[:, 512:1024]),
                    inc=("vm", 1))
                E.op("vector", lambda e, qi=qi: e.tensor_tensor(
                    sqb[:, 0:512], qraw[:, 0:512], sinS[:, qi * 512:(qi + 1) * 512],
                    ALU.mult), inc=("vm", 1))
                E.op("vector", lambda e, qi=qi: e.tensor_tensor(
                    sqb[:, 512:1024], qraw[:, 512:1024],
                    cosS[:, qi * 512:(qi + 1) * 512], ALU.mult), inc=("vm", 1))
                E.op("vector", lambda e, je=je: e.tensor_add(
                    Qb[:, (je + 4) * 512:(je + 5) * 512], sqb[:, 0:512],
                    sqb[:, 512:1024]), inc=("vm", 1))
                gs.use("vm", E.ct["vm"])
                G("attnO").use("vm", E.ct["vm"])
            vm_Q = E.ct["vm"]
            G("Qb").reset()

            for jq in range(4):
                for w2, jt in enumerate((jq, jq + 4)):
                    gs = G("sqb")
                    E.op("scalar", lambda e, jt=jt: e.activation(
                        sqb[:, 0:512], Qb[:, jt * 512:(jt + 1) * 512], AF.Square),
                        waits=[("vm", vm_Q)] + gs.waits(), inc=("sm", 1))
                    sm_s = E.ct["sm"]
                    E.op("tensor", lambda e, w2=w2: e.matmul(
                        PS[3][:, 0:512], e4S[:], sqb[:, 0:512],
                        start=(w2 == 0), stop=(w2 == 1)),
                        waits=[("sm", sm_s)] + (psG[3].waits() if w2 == 0 else []),
                        inc=("tm", 1))
                    gs.use("tm", E.ct["tm"])
                psG[3].reset()
                tm_qn = E.ct["tm"]
                E.op("scalar", lambda e: e.activation(
                    scrC[:, 0:512], PS[3][:, 0:512], AF.Sqrt),
                    waits=[("tm", tm_qn)] + G("scrC").waits(), inc=("sm", 1))
                sm_qn = E.ct["sm"]
                psG[3].use("sm", sm_qn)
                gc8 = G("wo64c8")
                E.op("vector", lambda e, jq=jq, qi=qi: e.tensor_scalar_mul(
                    c8b[:, jq * 512:(jq + 1) * 512], scrC[:, 0:512],
                    kprS[:, jq * 16 + 4 * qi + 3:jq * 16 + 4 * qi + 4]),
                    waits=[("sm", sm_qn), ("sm", sm_kpr)] + gc8.waits(), inc=("vm", 1))
                G("scrC").use("vm", E.ct["vm"])
            vm_c8 = E.ct["vm"]
            G("wo64c8").reset()
            if DEBUG_PRINT == 2 and s == 0 and qi == 0:
                E.op("vector", lambda e: e.print("km2", km2S[0:1, 0:8]),
                     waits=[("sm", sm_cb)])
                E.op("vector", lambda e: e.print("cb", cbS[0:1, 0:4]))
                E.op("vector", lambda e: e.print("Q0", sb["Qb"][0:1, 0:6]))
                E.op("vector", lambda e: e.print("K0", sb["Kb"][0:1, 0:6]))

            for hp in range(8):
                jq = hp // 2
                je, jo2 = jq, jq + 4
                ra = 32 * ((2 * hp) % 4)
                rb = 32 * ((2 * hp + 1) % 4)
                nch_q = 4 * (qi + 1)

                def emit_st(cc):
                    par = cc % 2
                    pq = PS[par]
                    pg = psG[par]
                    diag = cc >= 4 * qi
                    o = (cc - 4 * qi) * 128 if diag else 0
                    wts = pg.waits() + [("vm", vm_Q), ("vm", vm_c8)]
                    for rr, half_t in ((ra, 0), (rb, 1)):
                        co = half_t * 512
                        E.op("tensor", lambda e, rr=rr, cc=cc, o=o, pq=pq, co=co: e.matmul(
                            pq[:, co + o:co + 512],
                            Kb[rr:rr + 32, je * T + cc * 128:je * T + (cc + 1) * 128],
                            Qb[rr:rr + 32, je * 512 + o:(je + 1) * 512],
                            start=True, stop=False, tile_position=(rr, 0)),
                            waits=wts)
                        E.op("tensor", lambda e, rr=rr, cc=cc, o=o, pq=pq, co=co: e.matmul(
                            pq[:, co + o:co + 512],
                            Kb[rr:rr + 32, jo2 * T + cc * 128:jo2 * T + (cc + 1) * 128],
                            Qb[rr:rr + 32, jo2 * 512 + o:(jo2 + 1) * 512],
                            start=False, stop=False, tile_position=(rr, 0)))
                        E.op("tensor", lambda e, rr=rr, o=o, pq=pq, co=co: e.matmul(
                            pq[:, co + o:co + 512],
                            negb[rr:rr + 1, 0:128],
                            c8b[rr:rr + 1, jq * 512 + o:(jq + 1) * 512],
                            start=False, stop=True, tile_position=(rr, 0)),
                            inc=("tm", 1))
                    pg.reset()
                    return E.ct["tm"], o, diag, par

                def emit_expv(cc, tm_s, o, diag, par, last):
                    pq = PS[par]
                    pg = psG[par]
                    gP = G(f"P{par}")
                    if o == 0:
                        E.op("scalar", lambda e, par=par, pq=pq: e.activation(
                            Pb[:, par * 1024:(par + 1) * 1024], pq[:, 0:1024], AF.Exp,
                            scale=0.125),
                            waits=[("tm", tm_s)] + gP.waits(), inc=("sm", 1))
                    else:
                        E.op("scalar", lambda e, par=par, pq=pq, o=o: e.activation(
                            Pb[:, par * 1024 + o:par * 1024 + 512], pq[:, o:512],
                            AF.Exp, scale=0.125),
                            waits=[("tm", tm_s)] + gP.waits(), inc=("sm", 1))
                        E.op("scalar", lambda e, par=par, pq=pq, o=o: e.activation(
                            Pb[:, par * 1024 + 512 + o:par * 1024 + 1024],
                            pq[:, 512 + o:1024], AF.Exp, scale=0.125), inc=("sm", 1))
                    sm_p = E.ct["sm"]
                    pg.use("sm", sm_p)
                    vmw = []
                    if diag:
                        E.op("vector", lambda e, par=par, o=o: e.tensor_tensor(
                            Pb[:, par * 1024 + o:par * 1024 + o + 128],
                            Pb[:, par * 1024 + o:par * 1024 + o + 128], triS[:],
                            ALU.mult), waits=[("sm", sm_p)], inc=("vm", 1))
                        E.op("vector", lambda e, par=par, o=o: e.tensor_tensor(
                            Pb[:, par * 1024 + 512 + o:par * 1024 + 512 + o + 128],
                            Pb[:, par * 1024 + 512 + o:par * 1024 + 512 + o + 128],
                            triS[:], ALU.mult), inc=("vm", 1))
                        vmw = [("vm", E.ct["vm"])]
                    pvw = [("sm", sm_p)] + vmw
                    E.op("tensor", lambda e, cc=cc, o=o, par=par, last=last: e.matmul(
                        PS[2][0:65, o:512],
                        Vb[:, cc * 1040 + (2 * hp) * 65:cc * 1040 + (2 * hp) * 65 + 65],
                        Pb[:, par * 1024 + o:par * 1024 + 512],
                        start=(cc == 0), stop=last),
                        waits=pvw + (psG[2].waits() if cc == 0 else []), inc=None)
                    E.op("tensor", lambda e, cc=cc, o=o, par=par, last=last: e.matmul(
                        PS[2][0:65, 512 + o:1024],
                        Vb[:, cc * 1040 + (2 * hp + 1) * 65:cc * 1040 + (2 * hp + 1) * 65 + 65],
                        Pb[:, par * 1024 + 512 + o:par * 1024 + 1024],
                        start=(cc == 0), stop=last), inc=("tm", 1))
                    gP.use("tm", E.ct["tm"])
                    if cc == 0:
                        psG[2].reset()

                prev = None
                for cc in range(nch_q):
                    cur = (cc,) + emit_st(cc)
                    if prev is not None:
                        emit_expv(prev[0], prev[1], prev[2], prev[3], prev[4],
                                  prev[0] == nch_q - 1)
                    prev = cur
                emit_expv(prev[0], prev[1], prev[2], prev[3], prev[4], True)
                tm_pv = E.ct["tm"]
                if DEBUG_PRINT == 2 and s == 0 and qi == 0 and hp == 0:
                    E.op("vector", lambda e: e.print("P", Pb[0:1, 0:6]),
                         waits=[("tm", tm_pv)])
                    E.op("vector", lambda e: e.print("l", PS[2][64:65, 0:6]))
                    E.op("vector", lambda e: e.print("O", PS[2][0:1, 0:6]))

                for hh in range(2):
                    h = 2 * hp + hh
                    co = hh * 512
                    E.op("scalar", lambda e, co=co: e.activation(
                        srow[64:65, 0:512], PS[2][64:65, co:co + 512], AF.Ln),
                        waits=[("tm", tm_pv)] + G("srow64").waits(), inc=("sm", 1))
                    sm_ll = E.ct["sm"]
                    E.op("tensor", lambda e: e.matmul(
                        PS[3][0:64, 0:512], onesf[64:65, 0:64], srow[64:65, 0:512],
                        start=True, stop=True, tile_position=(64, 0)),
                        waits=[("sm", sm_ll)] + psG[3].waits(), inc=("tm", 1))
                    tm_rc = E.ct["tm"]
                    psG[3].reset()
                    G("srow64").use("tm", tm_rc)
                    E.op("scalar", lambda e, co=co: e.activation(
                        oscr[0:64, 0:512], PS[2][0:64, co:co + 512], AF.Identity),
                        waits=[("tm", tm_pv)] + G("oscr").waits(),
                        inc=("sm", 1))
                    E.op("scalar", lambda e: e.activation(
                        scrC[:, 0:512][0:64, :], PS[3][0:64, 0:512], AF.Exp,
                        scale=-1.0),
                        waits=[("tm", tm_rc)] + G("scrC").waits(), inc=("sm", 1))
                    sm_o = E.ct["sm"]
                    psG[3].use("sm", sm_o)
                    gA = G("attnO")
                    E.op("vector", lambda e, h=h: e.tensor_tensor(
                        attnO[0:64, h * 512:(h + 1) * 512], oscr[0:64, 0:512],
                        scrC[0:64, 0:512], ALU.mult),
                        waits=[("sm", sm_o)], inc=("vm", 1))
                    G("oscr").use("vm", E.ct["vm"])
                    G("scrC").use("vm", E.ct["vm"])
                    if hh == 1:
                        psG[2].use("sm", E.ct["sm"])
            vm_attnO = E.ct["vm"]
            G("wo64c8").use("tm", E.ct["tm"])

            for dj in range(NT):
                wb = wo64A
                gw = G("wo64c8")
                srcw_ = RH64["wo0T"][:, :, dj * 128:(dj + 1) * 128]
                dstw = wb[:, 0:2048].rearrange("p (h n) -> p h n", n=128)
                E.op("gpsimd", lambda e, d=dstw, srcw=srcw_: e.dma_start(out=d, in_=srcw),
                     waits=gw.waits(), inc=("wl", 16))
                wl_o = E.ct["wl"]
                for h in range(H):
                    E.op("tensor", lambda e, h=h, wb=wb: e.matmul(
                        PS[3][:, 0:512], wb[0:64, h * 128:(h + 1) * 128],
                        attnO[0:64, h * 512:(h + 1) * 512],
                        start=(h == 0), stop=(h == H - 1)),
                        waits=[("wl", wl_o), ("vm", vm_attnO)] +
                        (psG[3].waits() if h == 0 else []),
                        inc=(("tm", 1) if h == H - 1 else None))
                gw.use("tm", E.ct["tm"])
                psG[3].reset()
                tm_wo = E.ct["tm"]
                E.op("vector", lambda e, dj=dj, qi=qi: e.tensor_add(
                    stF[:, dj * T + qi * 512:dj * T + (qi + 1) * 512], PS[3][:, 0:512],
                    stF[:, dj * T + qi * 512:dj * T + (qi + 1) * 512]),
                    waits=[("tm", tm_wo)], inc=("vm", 1))
                psG[3].use("vm", E.ct["vm"])
            G("attnO").use("tm", E.ct["tm"])
        G("cosS").use("vm", E.ct["vm"])
        G("sinS").use("vm", E.ct["vm"])
        E.barrier()

        if DEBUG_TAPS and s == 0:
            for j in range(NT):
                E.op("sync", lambda e, j=j: e.dma_start(
                    out=taps["tap_state"][j * 128:(j + 1) * 128, :],
                    in_=stF[:, j * T:(j + 1) * T]),
                    waits=[("vm", E.ct["vm"]), ("tm", E.ct["tm"])], inc=("st", 16))

        E.op("gpsimd", lambda e: e.dma_start(
            out=cn1S[:].rearrange("p (k n) -> p k n", n=1024), in_=RH["cn1"][:, 0:8, :]),
            waits=G("cn1S").waits(), inc=("wl", 16))
        E.op("gpsimd", lambda e: e.dma_start(
            out=cn2S[:].rearrange("p (k n) -> p k n", n=512), in_=RH["cn2"][:, 0:8, :]),
            waits=G("cn2S").waits(), inc=("wl", 16))
        E.op("gpsimd", lambda e: e.dma_start(
            out=cn3S[:].rearrange("p (k n) -> p k n", n=256), in_=RH["cn3"][:, 0:4, :]),
            waits=G("cn3S").waits(), inc=("wl", 16))
        E.op("gpsimd", lambda e: e.dma_start(
            out=cn4S[:], in_=dram["cn4c"]),
            waits=G("cn4S").waits(), inc=("wl", 16))
        wl_cn = E.ct["wl"]

        for tt in range(NQT):
            gx = G("xre")
            E.op("sync", lambda e, tt=tt: e.dma_start(
                out=xre[:].rearrange("p (j t) -> p j t", t=512),
                in_=xre_src[s][:, :, tt * 512:(tt + 1) * 512]),
                waits=gx.waits(), inc=("ld", 16))
            ld_xre = E.ct["ld"]
            tm_rep = reps(tt * 512, fp32=True)
            gst = G("stT")
            gob = G("outb")
            for j in range(NT):
                E.op("vector", lambda e, j=j: e.tensor_sub(
                    scrC[:, 0:512], xre[:, j * 512:(j + 1) * 512], PS[3][:, 0:512]),
                    waits=[("tm", tm_rep), ("ld", ld_xre)] + G("scrC").waits() +
                    (gst.waits() if j == 0 else []), inc=("vm", 1))
                E.op("vector", lambda e, j=j: e.tensor_tensor(
                    stT[:, j * 512:(j + 1) * 512], scrC[:, 0:512], PS[3][:, 512:1024],
                    ALU.mult), inc=("vm", 1))
                E.op("vector", lambda e, j=j: e.tensor_copy(
                    outb[:, j * 512:(j + 1) * 512], stT[:, j * 512:(j + 1) * 512]),
                    waits=(gob.waits() if j == 0 else []), inc=("vm", 1))
            gst.reset(); gob.reset()
            psG[3].use("vm", E.ct["vm"])
            G("xre").use("vm", E.ct["vm"])

            for cyc in range(3):
                vm_in = E.ct["vm"]
                for dj in range(NT):
                    pq = PS[dj % 2]; pg = psG[dj % 2]
                    for kt in range(8):
                        E.op("tensor", lambda e, dj=dj, kt=kt, pq=pq: e.matmul(
                            pq[:, 0:512],
                            cn1S[:, kt * 1024 + dj * 128:kt * 1024 + (dj + 1) * 128],
                            outb[:, kt * 512:(kt + 1) * 512],
                            start=(kt == 0), stop=(kt == 7)),
                            waits=[("wl", wl_cn), ("vm", vm_in)] +
                            (pg.waits() if kt == 0 else []),
                            inc=(("tm", 1) if kt == 7 else None))
                    pg.reset()
                    tm_h = E.ct["tm"]
                    gh = G("h1b")
                    E.op("scalar", lambda e, dj=dj, pq=pq: e.activation(
                        h1b[:, dj * 512:(dj + 1) * 512], pq[:, 0:512], AF.Gelu,
                        bias=sb["b1S"][:, dj:dj + 1]),
                        waits=[("tm", tm_h)] + (gh.waits() if dj == 0 else []),
                        inc=("sm", 1))
                    pg.use("sm", E.ct["sm"])
                G("h1b").reset()
                sm_h1 = E.ct["sm"]
                for dj in range(4):
                    pq = PS[dj % 2]; pg = psG[dj % 2]
                    for kt in range(8):
                        E.op("tensor", lambda e, dj=dj, kt=kt, pq=pq: e.matmul(
                            pq[:, 0:512],
                            cn2S[:, kt * 512 + dj * 128:kt * 512 + (dj + 1) * 128],
                            h1b[:, kt * 512:(kt + 1) * 512],
                            start=(kt == 0), stop=(kt == 7)),
                            waits=[("sm", sm_h1)] + (pg.waits() if kt == 0 else []),
                            inc=(("tm", 1) if kt == 7 else None))
                    pg.reset()
                    tm_h = E.ct["tm"]
                    gh = G("h2b")
                    E.op("scalar", lambda e, dj=dj, pq=pq: e.activation(
                        h2b[:, dj * 512:(dj + 1) * 512], pq[:, 0:512], AF.Gelu,
                        bias=sb["b2S"][:, dj:dj + 1]),
                        waits=[("tm", tm_h)] + (gh.waits() if dj == 0 else []),
                        inc=("sm", 1))
                    pg.use("sm", E.ct["sm"])
                G("h2b").reset()
                sm_h2 = E.ct["sm"]
                G("h1b").use("tm", E.ct["tm"])
                for dj in range(2):
                    pq = PS[dj % 2]; pg = psG[dj % 2]
                    for kt in range(4):
                        E.op("tensor", lambda e, dj=dj, kt=kt, pq=pq: e.matmul(
                            pq[:, 0:512],
                            cn3S[:, kt * 256 + dj * 128:kt * 256 + (dj + 1) * 128],
                            h2b[:, kt * 512:(kt + 1) * 512],
                            start=(kt == 0), stop=(kt == 3)),
                            waits=[("sm", sm_h2)] + (pg.waits() if kt == 0 else []),
                            inc=(("tm", 1) if kt == 3 else None))
                    pg.reset()
                    tm_h = E.ct["tm"]
                    gh = G("h3b")
                    E.op("scalar", lambda e, dj=dj, pq=pq: e.activation(
                        h3b[:, dj * 512:(dj + 1) * 512], pq[:, 0:512], AF.Gelu,
                        bias=sb["b3S"][:, dj:dj + 1]),
                        waits=[("tm", tm_h)] + (gh.waits() if dj == 0 else []),
                        inc=("sm", 1))
                    pg.use("sm", E.ct["sm"])
                G("h3b").reset()
                sm_h3 = E.ct["sm"]
                G("h2b").use("tm", E.ct["tm"])
                for kt in range(2):
                    E.op("tensor", lambda e, kt=kt: e.matmul(
                        PS[3][0:1, 0:512], cn4S[:, kt:kt + 1],
                        h3b[:, kt * 512:(kt + 1) * 512],
                        start=(kt == 0), stop=(kt == 1)),
                        waits=[("sm", sm_h3)] + (psG[3].waits() if kt == 0 else []),
                        inc=(("tm", 1) if kt == 1 else None))
                psG[3].reset()
                tm_s4 = E.ct["tm"]
                G("h3b").use("tm", tm_s4)
                E.op("scalar", lambda e: e.activation(
                    srow[0:1, 0:512], PS[3][0:1, 0:512], AF.Sigmoid,
                    bias=smalls[0:1, 0:1]), waits=[("tm", tm_s4)], inc=("sm", 1))
                sm_tf = E.ct["sm"]
                psG[3].use("sm", sm_tf)
                E.op("vector", lambda e: e.tensor_scalar(
                    srow[0:1, 512:1024], srow[0:1, 0:512], smalls[0:1, 5:6],
                    smalls[0:1, 4:5], ALU.mult, ALU.add),
                    waits=[("sm", sm_tf), ("vm", vm_ab)], inc=("vm", 1))
                vm_cp = E.ct["vm"]
                E.op("tensor", lambda e: e.matmul(
                    PS[3][:, 512:1024], onesf[0:1, :], srow[0:1, 512:1024],
                    start=True, stop=True),
                    waits=[("vm", vm_cp)] + psG[3].waits(), inc=("tm", 1))
                tm_cp = E.ct["tm"]
                psG[3].reset()
                for pr in range(4):
                    jA, jB = pr, pr + 4
                    pos = pr < 2
                    E.op("vector", lambda e, jB=jB: e.tensor_tensor(
                        twsc[:, 0:512], PS[3][:, 512:1024],
                        stT[:, jB * 512:(jB + 1) * 512], ALU.mult),
                        waits=[("tm", tm_cp)], inc=("vm", 1))
                    E.op("vector", lambda e, jA=jA: e.tensor_tensor(
                        twsc[:, 512:1024], PS[3][:, 512:1024],
                        stT[:, jA * 512:(jA + 1) * 512], ALU.mult), inc=("vm", 1))
                    if pos:
                        E.op("vector", lambda e, jA=jA: e.tensor_add(
                            stT[:, jA * 512:(jA + 1) * 512],
                            stT[:, jA * 512:(jA + 1) * 512], twsc[:, 0:512]),
                            inc=("vm", 1))
                        E.op("vector", lambda e, jB=jB: e.tensor_sub(
                            stT[:, jB * 512:(jB + 1) * 512],
                            stT[:, jB * 512:(jB + 1) * 512], twsc[:, 512:1024]),
                            inc=("vm", 1))
                    else:
                        E.op("vector", lambda e, jA=jA: e.tensor_sub(
                            stT[:, jA * 512:(jA + 1) * 512],
                            stT[:, jA * 512:(jA + 1) * 512], twsc[:, 0:512]),
                            inc=("vm", 1))
                        E.op("vector", lambda e, jB=jB: e.tensor_add(
                            stT[:, jB * 512:(jB + 1) * 512],
                            stT[:, jB * 512:(jB + 1) * 512], twsc[:, 512:1024]),
                            inc=("vm", 1))
                psG[3].use("vm", E.ct["vm"])
                if cyc < 2:
                    for j in range(NT):
                        E.op("vector", lambda e, j=j: e.tensor_copy(
                            outb[:, j * 512:(j + 1) * 512],
                            stT[:, j * 512:(j + 1) * 512]), inc=("vm", 1))
            for j in range(NT):
                E.op("vector", lambda e, j=j, tt=tt: e.tensor_add(
                    stF[:, j * T + tt * 512:j * T + (tt + 1) * 512],
                    stF[:, j * T + tt * 512:j * T + (tt + 1) * 512],
                    stT[:, j * 512:(j + 1) * 512]), inc=("vm", 1))
            G("stT").use("vm", E.ct["vm"])
            G("outb").use("tm", E.ct["tm"])
        G("cn1S").use("tm", E.ct["tm"])
        G("cn2S").use("tm", E.ct["tm"])
        G("cn3S").use("tm", E.ct["tm"])
        G("cn4S").use("tm", E.ct["tm"])
        E.barrier()
        if DEBUG_TAPS and s == 0:
            for j in range(NT):
                E.op("sync", lambda e, j=j: e.dma_start(
                    out=taps["tap_x1"][j * 128:(j + 1) * 128, :],
                    in_=stF[:, j * T:(j + 1) * T]),
                    waits=[("vm", E.ct["vm"]), ("tm", E.ct["tm"])], inc=("st", 16))
            E.op("sync", lambda e: e.dma_start(
                out=taps["tap_rows"][0:1, 2 * T:2 * T + 8], in_=sb["smalls"][0:1, 0:8]),
                inc=("st", 16))

        ln_stats(sqb1, "sqb1")
        vm_rows = E.ct["vm"]
        for tc in range(4):
            tm_rep = reps(tc * 512)
            for j in range(NT):
                gx = G("xnb2")
                E.op("vector", lambda e, j=j, tc=tc: e.tensor_sub(
                    scrC[:, 0:512], stF[:, j * T + tc * 512:j * T + (tc + 1) * 512],
                    PS[3][:, 0:512]),
                    waits=[("tm", tm_rep)] + G("scrC").waits() +
                    (gx.waits() if j == 0 and tc == 0 else []), inc=("vm", 1))
                E.op("vector", lambda e, j=j, tc=tc: e.tensor_tensor(
                    xnb2[:, j * T + tc * 512:j * T + (tc + 1) * 512], scrC[:, 0:512],
                    PS[3][:, 512:1024], ALU.mult), inc=("vm", 1))
            psG[3].use("vm", E.ct["vm"])
        G("xnb2").reset()
        vm_xnb2 = E.ct["vm"]

        g = G("selS")
        E.op("sync", lambda e: e.dma_start(out=selS[:], in_=dram["sel"]),
             waits=g.waits(), inc=("ld", 16))
        g.reset()
        g = G("bvrow")
        E.op("sync", lambda e: e.dma_start(out=bvrow[:], in_=dram["bv1r"]),
             waits=g.waits(), inc=("ld", 16))
        g.reset()
        ld_tab = E.ct["ld"]
        for hf in range(2):
            E.op("tensor", lambda e, hf=hf: e.matmul(
                PS[3][:, hf * 512:(hf + 1) * 512], onesf[0:1, :],
                bvrow[0:1, hf * 512:(hf + 1) * 512], start=True, stop=True),
                waits=[("vm", vm_ones), ("ld", ld_tab)] +
                (psG[3].waits() if hf == 0 else []), inc=("tm", 1))
        psG[3].reset()
        G("bvrow").use("tm", E.ct["tm"])
        E.op("scalar", lambda e: e.activation(bvrS[:], PS[3][:, 0:1024], AF.Identity),
             waits=[("tm", E.ct["tm"])], inc=("sm", 1))
        psG[3].use("sm", E.ct["sm"])
        for tc in range(4):
            gi = G("scr1")
            E.op("gpsimd", lambda e, tc=tc: e.iota(
                scr1[0:16, 0:512], [[1, 512]], base=tc * 512, channel_multiplier=0,
                allow_small_or_imprecise_dtypes=True),
                waits=gi.waits(), inc=("gp", 1))
            gp_io = E.ct["gp"]
            E.op("scalar", lambda e, tc=tc: e.activation(
                decS[:, tc * 512:(tc + 1) * 512], scr1[0:16, 0:512], AF.Exp,
                scale=gamS[:, 1:2]),
                waits=[("gp", gp_io), ("sm", sm_lng)] + G("decS").waits(),
                inc=("sm", 1))
            gi.use("sm", E.ct["sm"])
        G("decS").reset()
        sm_dec = E.ct["sm"]
        E.barrier()

        if DEBUG_TAPS and s == 0:
            E.op("gpsimd", lambda e: e.dma_start(out=taps["tap_xnb"], in_=xnb2[:]),
                 waits=[("vm", vm_xnb2)], inc=("tp", 16))

        for half in range(2):
            jt0 = half * 4
            for tc in range(4):
                for jo in range(4):
                    jglob = jt0 + jo
                    wb = wsl1[jo % 2]
                    gw = G(f"wsl1{jo % 2}")
                    wl_k = wload(wb, "wk1T", jglob * 128, 128, waits=gw.waits())
                    pq = PS[jo % 2]; pg = psG[jo % 2]
                    for kt in range(8):
                        E.op("tensor", lambda e, kt=kt, tc=tc, wb=wb, pq=pq: e.matmul(
                            pq[:, 0:512], wb[:, kt * 128:(kt + 1) * 128],
                            xnb2[:, kt * T + tc * 512:kt * T + (tc + 1) * 512],
                            start=(kt == 0), stop=(kt == 7)),
                            waits=[("wl", wl_k), ("vm", vm_xnb2)] +
                            (pg.waits() if kt == 0 else []),
                            inc=(("tm", 1) if kt == 7 else None))
                    gw.use("tm", E.ct["tm"])
                    pg.reset()
                    tm_k = E.ct["tm"]
                    gk = G("K2b")
                    E.op("scalar", lambda e, jo=jo, tc=tc, jglob=jglob, pq=pq: e.activation(
                        K2b[:, jo * T + tc * 512:jo * T + (tc + 1) * 512], pq[:, 0:512],
                        AF.Identity, bias=sb["bk1S"][:, jglob:jglob + 1]),
                        waits=[("tm", tm_k)] + (gk.waits() if jo == 0 and tc == 0 else []),
                        inc=("sm", 1))
                    pg.use("sm", E.ct["sm"])
                wl_v = wload(wsl1[2], "wv1T", half * 512, 256, waits=G("wsl12").waits())
                wl_v2 = wload(wsl1[3], "wv1T", half * 512 + 256, 256, waits=G("wsl13").waits())
                for ts_ in range(4):
                    tsub = tc * 4 + ts_
                    for qr in range(2):
                        wb = wsl1[2 + qr]
                        wl_use = wl_v if qr == 0 else wl_v2
                        pq = PS[ts_ % 2]; pg = psG[ts_ % 2]
                        for kt in range(8):
                            E.op("tensor", lambda e, kt=kt, ts_=ts_, tc=tc, wb=wb, qr=qr, pq=pq: e.matmul(
                                pq[:, 0:256],
                                xnb2[:, kt * T + tc * 512 + ts_ * 128:kt * T + tc * 512 + (ts_ + 1) * 128],
                                wb[:, kt * 256:(kt + 1) * 256],
                                start=(kt == 0), stop=(kt == 7)),
                                waits=[("wl", wl_use), ("vm", vm_xnb2)] +
                                (pg.waits() if kt == 0 else []),
                                inc=(("tm", 1) if kt == 7 else None))
                        pg.reset()
                        tm_v = E.ct["tm"]
                        gv = G("V2b")
                        E.op("vector", lambda e, tsub=tsub, qr=qr, half=half, pq=pq: e.tensor_tensor(
                            V2b[:, tsub * 512 + qr * 256:tsub * 512 + (qr + 1) * 256],
                            pq[:, 0:256],
                            bvrS[:, half * 512 + qr * 256:half * 512 + (qr + 1) * 256],
                            ALU.add),
                            waits=[("tm", tm_v)] + (gv.waits() if tsub == 0 and qr == 0 else []),
                            inc=("vm", 1))
                        pg.use("vm", E.ct["vm"])
                G("wsl12").use("tm", E.ct["tm"])
                G("wsl13").use("tm", E.ct["tm"])
            G("K2b").reset(); G("V2b").reset()
            vm_V2 = E.ct["vm"]
            sm_K2 = E.ct["sm"]

            for qi in range(NQT):
                for jo in range(4):
                    jglob = jt0 + jo
                    wb = wsl1[jo % 2]
                    gw = G(f"wsl1{jo % 2}")
                    wl_q = wload(wb, "wq1T", jglob * 128, 128, waits=gw.waits())
                    for kt in range(8):
                        E.op("tensor", lambda e, kt=kt, qi=qi, wb=wb: e.matmul(
                            PS[3][:, 0:512], wb[:, kt * 128:(kt + 1) * 128],
                            xnb2[:, kt * T + qi * 512:kt * T + (qi + 1) * 512],
                            start=(kt == 0), stop=(kt == 7)),
                            waits=[("wl", wl_q), ("vm", vm_xnb2)] +
                            (psG[3].waits() if kt == 0 else []),
                            inc=(("tm", 1) if kt == 7 else None))
                    gw.use("tm", E.ct["tm"])
                    psG[3].reset()
                    tm_q = E.ct["tm"]
                    E.op("scalar", lambda e, jglob=jglob: e.activation(
                        scr1[:, 0:512], PS[3][:, 0:512], AF.Identity,
                        bias=sb["bq1S"][:, jglob:jglob + 1]),
                        waits=[("tm", tm_q)] + G("scr1").waits(), inc=("sm", 1))
                    sm_qr = E.ct["sm"]
                    psG[3].use("sm", sm_qr)
                    E.op("tensor", lambda e, jglob=jglob, qi=qi: e.matmul(
                        PS[2][:, 0:512], selS[:, jglob * 128:(jglob + 1) * 128],
                        decS[:, qi * 512:(qi + 1) * 512], start=True, stop=True),
                        waits=[("sm", sm_dec)] + psG[2].waits(), inc=("tm", 1))
                    tm_dr = E.ct["tm"]
                    psG[2].reset()
                    gq2 = G("Q2b")
                    E.op("vector", lambda e, jo=jo: e.tensor_tensor(
                        Q2b[:, jo * 512:(jo + 1) * 512], scr1[:, 0:512],
                        PS[2][:, 0:512], ALU.mult),
                        waits=[("sm", sm_qr), ("tm", tm_dr)] +
                        (gq2.waits() if jo == 0 else []), inc=("vm", 1))
                    G("scr1").use("vm", E.ct["vm"])
                    psG[2].use("vm", E.ct["vm"])
                vm_Q2 = E.ct["vm"]
                G("Q2b").reset()

                for jo in range(4):
                    nch_q = 4 * (qi + 1)

                    def emit_st1(cc):
                        par = cc % 2
                        pq = PS[par]; pg = psG[par]
                        diag = cc >= 4 * qi
                        o = (cc - 4 * qi) * 128 if diag else 0
                        E.op("tensor", lambda e, jo=jo, cc=cc, o=o, pq=pq: e.matmul(
                            pq[:, o:512],
                            K2b[0:64, jo * T + cc * 128:jo * T + (cc + 1) * 128],
                            Q2b[0:64, jo * 512 + o:(jo + 1) * 512],
                            start=True, stop=True),
                            waits=pg.waits() + [("vm", vm_Q2), ("sm", sm_K2)])
                        E.op("tensor", lambda e, jo=jo, cc=cc, o=o, pq=pq: e.matmul(
                            pq[:, 512 + o:1024],
                            K2b[64:128, jo * T + cc * 128:jo * T + (cc + 1) * 128],
                            Q2b[64:128, jo * 512 + o:(jo + 1) * 512],
                            start=True, stop=True, tile_position=(64, 0)),
                            inc=("tm", 1))
                        pg.reset()
                        return E.ct["tm"], o, diag, par

                    def emit_cpv1(cc, tm_s, o, diag, par, last):
                        pq = PS[par]; pg = psG[par]
                        gP = G(f"P1{par}")
                        E.op("scalar", lambda e, par=par, pq=pq, o=o: e.activation(
                            P1b[:, par * 1024 + o:par * 1024 + 512], pq[:, o:512],
                            AF.Identity), waits=[("tm", tm_s)] + gP.waits(),
                            inc=("sm", 1))
                        E.op("vector", lambda e, par=par, pq=pq, o=o: e.tensor_copy(
                            P1b[:, par * 1024 + 512 + o:par * 1024 + 1024],
                            pq[:, 512 + o:1024]),
                            waits=[("tm", tm_s)] + gP.waits(), inc=("vm", 1))
                        sm_p = E.ct["sm"]
                        vm_p = E.ct["vm"]
                        pg.use("sm", sm_p)
                        pg.use("vm", vm_p)
                        vmw = [("vm", vm_p)]
                        if diag:
                            E.op("vector", lambda e, par=par, o=o: e.tensor_tensor(
                                P1b[:, par * 1024 + o:par * 1024 + o + 128],
                                P1b[:, par * 1024 + o:par * 1024 + o + 128], triS[:],
                                ALU.mult), waits=[("sm", sm_p)], inc=("vm", 1))
                            E.op("vector", lambda e, par=par, o=o: e.tensor_tensor(
                                P1b[:, par * 1024 + 512 + o:par * 1024 + 512 + o + 128],
                                P1b[:, par * 1024 + 512 + o:par * 1024 + 512 + o + 128],
                                triS[:], ALU.mult), inc=("vm", 1))
                            vmw = [("vm", E.ct["vm"])]
                        pvw = [("sm", sm_p), ("vm", vm_V2)] + vmw
                        E.op("tensor", lambda e, cc=cc, jo=jo, par=par, o=o, last=last: e.matmul(
                            PS[2][0:64, o:512],
                            V2b[:, cc * 512 + (2 * jo) * 64:cc * 512 + (2 * jo) * 64 + 64],
                            P1b[:, par * 1024 + o:par * 1024 + 512],
                            start=(cc == 0), stop=last),
                            waits=pvw + (psG[2].waits() if cc == 0 else []))
                        E.op("tensor", lambda e, cc=cc, jo=jo, par=par, o=o, last=last: e.matmul(
                            PS[2][64:128, 512 + o:1024],
                            V2b[:, cc * 512 + (2 * jo + 1) * 64:cc * 512 + (2 * jo + 1) * 64 + 64],
                            P1b[:, par * 1024 + 512 + o:par * 1024 + 1024],
                            start=(cc == 0), stop=last, tile_position=(0, 64)),
                            inc=("tm", 1))
                        gP.use("tm", E.ct["tm"])
                        if cc == 0:
                            psG[2].reset()

                    prev = None
                    for cc in range(nch_q):
                        cur = (cc,) + emit_st1(cc)
                        if prev is not None:
                            emit_cpv1(prev[0], prev[1], prev[2], prev[3], prev[4],
                                      prev[0] == nch_q - 1)
                        prev = cur
                    emit_cpv1(prev[0], prev[1], prev[2], prev[3], prev[4], True)
                    tm_pv = E.ct["tm"]
                    gA = G("aO1")
                    E.op("scalar", lambda e, jo=jo: e.activation(
                        aO1[0:64, jo * 1024:jo * 1024 + 512],
                        PS[2][0:64, 0:512], AF.Identity),
                        waits=[("tm", tm_pv)] + (gA.waits() if jo == 0 else []),
                        inc=("sm", 1))
                    E.op("scalar", lambda e, jo=jo: e.activation(
                        aO1[64:128, jo * 1024:jo * 1024 + 512],
                        PS[2][64:128, 512:1024], AF.Identity), inc=("sm", 1))
                    psG[2].use("sm", E.ct["sm"])
                G("aO1").reset()
                sm_aO = E.ct["sm"]

                for dj in range(NT):
                    wb = wsl1[2 + dj % 2]
                    gw = G(f"wsl1{2 + dj % 2}")
                    wl_o = wload(wb, "wo1T", dj * 128, 128, waits=gw.waits())
                    for kk in range(4):
                        kt = jt0 + kk
                        E.op("tensor", lambda e, kk=kk, kt=kt, wb=wb: e.matmul(
                            PS[3][:, 0:512], wb[:, kt * 128:(kt + 1) * 128],
                            aO1[:, kk * 1024:kk * 1024 + 512],
                            start=(kk == 0), stop=(kk == 3)),
                            waits=[("wl", wl_o), ("sm", sm_aO)] +
                            (psG[3].waits() if kk == 0 else []),
                            inc=(("tm", 1) if kk == 3 else None))
                    gw.use("tm", E.ct["tm"])
                    psG[3].reset()
                    tm_wo = E.ct["tm"]
                    if half == 0:
                        E.op("vector", lambda e, dj=dj, qi=qi: e.scalar_tensor_tensor(
                            stF[:, dj * T + qi * 512:dj * T + (qi + 1) * 512],
                            PS[3][:, 0:512], sb["bo1S"][:, dj:dj + 1],
                            stF[:, dj * T + qi * 512:dj * T + (qi + 1) * 512],
                            ALU.add, ALU.add), waits=[("tm", tm_wo)], inc=("vm", 1))
                    else:
                        E.op("vector", lambda e, dj=dj, qi=qi: e.tensor_add(
                            stF[:, dj * T + qi * 512:dj * T + (qi + 1) * 512],
                            PS[3][:, 0:512],
                            stF[:, dj * T + qi * 512:dj * T + (qi + 1) * 512]),
                            waits=[("tm", tm_wo)], inc=("vm", 1))
                    psG[3].use("vm", E.ct["vm"])
                G("aO1").use("tm", E.ct["tm"])
            G("K2b").use("tm", E.ct["tm"])
            G("V2b").use("tm", E.ct["tm"])
        G("selS").use("tm", E.ct["tm"])
        E.barrier()

        wl_slot = {}

        def load_triple(hc):
            slot = hc % 9
            gw = G(f"ring{slot}")
            wload(ringG[slot], "wg1T", hc * 128, 128, waits=gw.waits())
            wload(ringU[slot], "wu1T", hc * 128, 128)
            E.op("gpsimd", lambda e, hc=hc, slot=slot: e.dma_start(
                out=ringD[slot][:, 0:1024], in_=RH["wd1T"][:, hc, 0:1024]),
                inc=("wl", 16))
            wl_slot[slot] = E.ct["wl"]
            gw.reset()

        for hc in range(6):
            load_triple(hc)
        for blk in range(4):
            if blk < 3:
                for hc in range(blk * 6 + 6, blk * 6 + 9):
                    load_triple(hc)
            for tt in range(NQT):
                hcs = [blk * 6 + i for i in range(6)]
                for i, hc in enumerate(hcs):
                    slot = hc % 9
                    pq = PS[i % 2]; pg = psG[i % 2]
                    for kt in range(8):
                        E.op("tensor", lambda e, kt=kt, slot=slot, tt=tt, pq=pq: e.matmul(
                            pq[:, 0:512], ringG[slot][:, kt * 128:(kt + 1) * 128],
                            xnb2[:, kt * T + tt * 512:kt * T + (tt + 1) * 512],
                            start=(kt == 0), stop=(kt == 7)),
                            waits=[("wl", wl_slot[slot]), ("vm", vm_xnb2)] +
                            (pg.waits() if kt == 0 else []),
                            inc=(("tm", 1) if kt == 7 else None))
                    pg.reset()
                    tm_g = E.ct["tm"]
                    gf_ = G("fscr")
                    E.op("scalar", lambda e, pq=pq: e.activation(
                        fscr[:, 0:512], pq[:, 0:512], AF.Silu, scale=S127),
                        waits=[("tm", tm_g)] + gf_.waits(), inc=("sm", 1))
                    sm_si = E.ct["sm"]
                    pg.use("sm", sm_si)
                    pq2 = PS[(i + 1) % 2]; pg2 = psG[(i + 1) % 2]
                    for kt in range(8):
                        E.op("tensor", lambda e, kt=kt, slot=slot, tt=tt, pq2=pq2: e.matmul(
                            pq2[:, 0:512], ringU[slot][:, kt * 128:(kt + 1) * 128],
                            xnb2[:, kt * T + tt * 512:kt * T + (tt + 1) * 512],
                            start=(kt == 0), stop=(kt == 7)),
                            waits=[("wl", wl_slot[slot])] + (pg2.waits() if kt == 0 else []),
                            inc=(("tm", 1) if kt == 7 else None))
                    pg2.reset()
                    tm_u = E.ct["tm"]
                    gh = G("hblk")
                    E.op("vector", lambda e, i=i, pq2=pq2: e.tensor_tensor(
                        hblk[:, i * 512:(i + 1) * 512], fscr[:, 0:512], pq2[:, 0:512],
                        ALU.mult),
                        waits=[("tm", tm_u), ("sm", sm_si)] +
                        (gh.waits() if i == 0 else []), inc=("vm", 1))
                    pg2.use("vm", E.ct["vm"])
                    G("fscr").use("vm", E.ct["vm"])
                G("hblk").reset()
                vm_h = E.ct["vm"]
                for dj in range(NT):
                    for i, hc in enumerate(hcs):
                        slot = hc % 9
                        E.op("tensor", lambda e, i=i, slot=slot, dj=dj: e.matmul(
                            PS[3][:, 0:512], ringD[slot][:, dj * 128:(dj + 1) * 128],
                            hblk[:, i * 512:(i + 1) * 512],
                            start=(i == 0), stop=(i == 5)),
                            waits=[("wl", wl_slot[slot]), ("vm", vm_h)] +
                            (psG[3].waits() if i == 0 else []),
                            inc=(("tm", 1) if i == 5 else None))
                    psG[3].reset()
                    tm_d = E.ct["tm"]
                    E.op("vector", lambda e, dj=dj, tt=tt: e.scalar_tensor_tensor(
                        stF[:, dj * T + tt * 512:dj * T + (tt + 1) * 512],
                        PS[3][:, 0:512], S127 * S127,
                        stF[:, dj * T + tt * 512:dj * T + (tt + 1) * 512],
                        ALU.mult, ALU.add), waits=[("tm", tm_d)], inc=("vm", 1))
                    psG[3].use("vm", E.ct["vm"])
                G("hblk").use("tm", E.ct["tm"])
            for hc in hcs:
                G(f"ring{hc % 9}").use("tm", E.ct["tm"])
            if blk < 3:
                for hc in range(blk * 6 + 9, blk * 6 + 12):
                    load_triple(hc)
        E.barrier()

        if DEBUG_TAPS and s == 0:
            for j in range(NT):
                E.op("sync", lambda e, j=j: e.dma_start(
                    out=taps["tap_state2"][j * 128:(j + 1) * 128, :],
                    in_=stF[:, j * T:(j + 1) * T]),
                    waits=[("vm", E.ct["vm"]), ("tm", E.ct["tm"])], inc=("st", 16))

        vm_fin, tm_fin = E.ct["vm"], E.ct["tm"]
        for j in range(NT):
            g = G(f"stF{j}")
            E.op("sync", lambda e, j=j: e.dma_start(
                out=out[s, j * 128:(j + 1) * 128, :], in_=stF[:, j * T:(j + 1) * T]),
                waits=[("vm", vm_fin), ("tm", tm_fin)], inc=("st", 16))
            g.use("st", E.ct["st"])

    E.waitall("sync", ("st",))
    if DEBUG_TAPS:
        E.waitall("gpsimd", ("tp",))



_NC_CACHE = {}


def kernel(**inputs):
    x = np.asarray(inputs["L0_attn_x"], dtype=np.float32)
    shared = stage_shared(inputs)
    in_maps = []
    for cidx in range(NCORES):
        m = dict(shared)
        m["xin"] = np.ascontiguousarray(x[2 * cidx:2 * cidx + 2].transpose(0, 2, 1))
        in_maps.append(m)
    if "nc" not in _NC_CACHE:
        _NC_CACHE["nc"] = build_nc()
    res = run_bass_kernel_spmd(_NC_CACHE["nc"], in_maps, core_ids=list(range(NCORES)))
    outs = [np.asarray(res.results[cidx]["out"]).transpose(0, 2, 1)
            for cidx in range(NCORES)]
    return np.concatenate(outs, axis=0).astype(np.float32)


if __name__ == "__main__":
    print("building program ...")
    build_nc()
    print("build ok")
